# version 06
# baseline (speedup 1.0000x reference)
import os
import numpy as np
import concourse.bass as bass
import concourse.mybir as mybir
from concourse.bass_utils import run_bass_kernel_spmd
from contextlib import ExitStack

F32 = mybir.dt.float32
BF16 = mybir.dt.bfloat16
I32 = mybir.dt.int32
AF = mybir.ActivationFunctionType
ALU = mybir.AluOpType
AX = mybir.AxisListType

NC = 8
T = 1024
D = 2048
NH = 16
DK = 128
CH = 64
NCH = T // CH
DFF = 8192
WIN = 1024
NT = T // 128
NSUB = int(os.environ.get("MK_SUBLAYERS", "6"))

LN_EPS = 1e-5
FP8_MAX = 448.0
SCALE_EPS = 1e-7
QB = 127.0
GEPS = 1e-5
RNE_C = 3.0 * (2.0 ** 22)
SWA_BIAS = -16.0



class Sem:
    def __init__(self, h):
        self.h = h
        self.n = 0


class DStream:

    def __init__(self, b, name, nslots, queue="sync", queues=None):
        self.b = b
        self.queues = queues or [queue] * nslots
        self.sems = [b.sem(f"{name}{i}") for i in range(nslots)]
        self.n = nslots
        self.i = 0
        self.free_at = [None] * nslots

    def load(self, pairs):
        p = self.i % self.n
        if self.free_at[p] is not None:
            s, c = self.free_at[p]
            self.b.wait(self.queues[p], s, c)
            self.free_at[p] = None
        for out, in_ in pairs:
            self.b.dma(self.queues[p], out, in_, self.sems[p])
        self.i += 1
        return (p, self.sems[p].n)

    def wait_tok(self, eng, tok):
        p, c = tok
        self.b.wait(eng, self.sems[p], c)

    def mark_free(self, tok, sem, count=None):
        p, _ = tok
        self.free_at[p] = (sem, sem.n if count is None else count)


class Builder:
    def __init__(self):
        self.nc = bass.Bass("TRN2", target_bir_lowering=False, debug=False)
        self.es = ExitStack()
        self.sched = []
        self.sems = {}

    def sem(self, name):
        s = Sem(self.es.enter_context(self.nc.semaphore(name)))
        self.sems[name] = s
        return s

    def sb(self, name, shape, dt=F32):
        return self.es.enter_context(self.nc.sbuf_tensor(name, shape, dt))

    def ps(self, name, shape, dt=F32):
        return self.es.enter_context(self.nc.psum_tensor(name, shape, dt))

    def raw(self, eng, fn):
        self.sched.append((eng, fn))

    def wait(self, eng, sem, val=None):
        v = sem.n if val is None else val
        if v <= 0:
            return
        self.raw(eng, lambda e, s=sem.h, v=v: e.wait_ge(s, v))

    def dma(self, eng, out, in_, sem):
        sem.n += 16
        self.raw(eng, lambda e, o=out, i=in_, s=sem.h: e.dma_start(out=o, in_=i).then_inc(s, 16))

    def op(self, eng, meth, sem=None, **kw):
        if sem is not None:
            sem.n += 1
        def f(e, m=meth, kw=kw, s=sem):
            r = getattr(e, m)(**kw)
            if s is not None:
                r.then_inc(s.h, 1)
        self.raw(eng, f)

    def drain(self, eng):
        self.raw(eng, lambda e: e.drain())

    def aux_dma(self, q, sem, out, in_):
        self.dma(q, out, in_, sem)
        self.wait(q, sem)
        return sem.n

    def finish(self):
        nc = self.nc
        sched = self.sched
        with nc.Block() as block:
            @block.sync
            def _(eng):
                for n, f in sched:
                    if n == "sync":
                        f(eng)

            @block.vector
            def _(eng):
                for n, f in sched:
                    if n == "vector":
                        f(eng)

            @block.scalar
            def _(eng):
                for n, f in sched:
                    if n == "scalar":
                        f(eng)

            @block.tensor
            def _(eng):
                for n, f in sched:
                    if n == "tensor":
                        f(eng)

            @block.gpsimd
            def _(eng):
                for n, f in sched:
                    if n == "gpsimd":
                        f(eng)
        self.es.close()
        return nc



def build_program():
    b = Builder()
    nc = b.nc

    def din(name, shape, dt=F32):
        return nc.dram_tensor(name, shape, dt, kind="ExternalInput").ap()

    x_d = din("x", [T, D])
    l0wT_d = din("l0wT", [D, D])
    l0ws_d = din("l0ws", [1, 1])
    l0bias_d = din("l0bias", [D])
    wq_d = din("wq", [D, D])
    wk_d = din("wk", [D, D])
    wi_d = din("wi", [D, D])
    wo_d = din("wo", [D, D])
    gw_d = din("gw", [D])
    awq_d = din("awq", [D, D])
    awk_d = din("awk", [D, D])
    awv_d = din("awv", [D, D])
    awo_d = din("awo", [D, D])
    w1T_d = din("w1T", [D, DFF])
    w2T_d = din("w2T", [D, DFF])
    w3T_d = din("w3T", [DFF, D])
    ident_d = din("ident", [128, 128], BF16)
    glamask_d = din("glamask", [CH, CH], BF16)
    swamask_d = din("swamask", [NT * 128, 1152], BF16)
    alphas_d = din("alphas", [8])
    oddsel_d = din("oddsel", [2])
    nbr_d = din("nbr", [8])
    pos_d = din("pos", [2 * T])
    iota64_d = din("iota64", [64, 1])

    out_d = nc.dram_tensor("out", [T, D], F32, kind="ExternalOutput").ap()

    ccs_in = nc.dram_tensor("ccs_in", [1, 8], F32)
    ccs_out = nc.dram_tensor("ccs_out", [1, 8], F32)
    ccg_in = nc.dram_tensor("ccg_in", [128, 16 + D], F32)
    ccg_out = nc.dram_tensor("ccg_out", [8 * 128, 16 + D], F32)
    cck_in = nc.dram_tensor("cck_in", [D, T], BF16)
    ccv_in = nc.dram_tensor("ccv_in", [T, D], BF16)
    cck_outF = nc.dram_tensor("cck_outF", [8 * D, T], BF16)
    ccv_outF = nc.dram_tensor("ccv_outF", [8 * T, D], BF16)
    scr_q = nc.dram_tensor("scr_q", [D, T], BF16)
    scr_at = nc.dram_tensor("scr_at", [D, T], BF16)

    X = b.sb("X", [128, NT * D])
    A = b.sb("A", [128, 8192])
    B = b.sb("B", [128, 8192])
    Cb = b.sb("Cb", [128, 8192])
    Db = b.sb("Db", [128, 8192])
    E = b.sb("E", [128, 2048])
    S1 = b.sb("S1", [128, 384])
    S2 = b.sb("S2", [128, 384])
    KON = b.sb("KON", [128, 256])
    SI = b.sb("SI", [128, 64], I32)
    S3 = b.sb("S3", [128, 384])

    Bh = B[:].bitcast(BF16)
    Ch = Cb[:].bitcast(BF16)
    Dh = Db[:].bitcast(BF16)
    Ah = A[:].bitcast(BF16)
    Eh = E[:].bitcast(BF16)

    PA0 = b.ps("PA0", [128, 1024])
    PA1 = b.ps("PA1", [128, 1024])
    PB = b.ps("PB", [128, 1024])
    PT0 = b.ps("PT0", [128, 1024], BF16)
    PT1 = b.ps("PT1", [128, 1024], BF16)

    s_syn = b.sem("s_syn")
    s_gpd = b.sem("s_gpd")
    s_scd = b.sem("s_scd")
    s_vec = b.sem("s_vec")
    s_sca = b.sem("s_sca")
    s_ten = b.sem("s_ten")
    s_gpc = b.sem("s_gpc")
    s_cc = b.sem("s_cc")
    s_out = b.sem("s_out")
    s_aux = b.sem("s_aux")

    wait, dma, op, raw, drain = b.wait, b.dma, b.op, b.raw, b.drain
    wst = DStream(b, "wst", 2, queues=["sync", "gpsimd"])

    def vts(out, in0, s1, s2, op0, op1=None, sem=None, eng="vector", accum=None):
        kw = dict(out=out, in0=in0, scalar1=s1, scalar2=s2, op0=op0)
        if op1 is not None:
            kw["op1"] = op1
        if accum is not None:
            kw["accum_out"] = accum
        op(eng, "tensor_scalar", sem=sem, **kw)

    def vtt(out, in0, in1, alu, sem=None, eng="vector"):
        op(eng, "tensor_tensor", sem=sem, out=out, in0=in0, in1=in1, op=alu)

    def vstt(out, in0, scal, in1, op0, op1, sem=None, eng="vector"):
        op(eng, "scalar_tensor_tensor", sem=sem, out=out, in0=in0, scalar=scal,
           in1=in1, op0=op0, op1=op1)

    def act(out, in_, func, bias=0.0, scale=1.0, accum=None, sem=None):
        kw = dict(out=out, in_=in_, func=func, bias=bias, scale=scale)
        if accum is not None:
            kw["accum_out"] = accum
        op("scalar", "activation", sem=sem, **kw)

    def mm(o, lhsT, rhs, start, stop, sem=None):
        op("tensor", "matmul", sem=sem, out=o, lhsT=lhsT, rhs=rhs,
           start=start, stop=stop)

    def transpose(o, in_, sem=None):
        op("tensor", "transpose", sem=sem, out=o, in_=in_, identity=identI)

    KONh = KON[:].bitcast(BF16)
    identI = KONh[:, 0:128]
    glam1 = KONh[0:CH, 128:128 + CH]

    def xt(r):
        return X[:, r * D:(r + 1) * D]

    def rsqrt_small(dst, src, n, eps):
        vts(S2[:, 256:256 + n], src, eps, None, ALU.add)
        drain("vector")
        vi = S2[:, 256:256 + n].bitcast(I32)
        vts(SI[:, 0:n], vi, 1, None, ALU.logical_shift_right)
        drain("vector")
        vts(SI[:, 0:n], SI[:, 0:n], -1, 0x5F3759DF, ALU.mult, ALU.add)
        drain("vector")
        y0 = SI[:, 0:n].bitcast(F32)
        v = S2[:, 256:256 + n]
        cur = y0
        for it in range(3):
            t1 = S2[:, 300:300 + n]
            vtt(t1, cur, cur, ALU.mult)
            drain("vector")
            vstt(t1, t1, -0.5, v, ALU.mult, ALU.mult)
            drain("vector")
            vts(t1, t1, 1.5, None, ALU.add)
            drain("vector")
            nxt = dst if it == 2 else S2[:, 340:340 + n]
            vtt(nxt, cur, t1, ALU.mult)
            drain("vector")
            cur = nxt

    for r in range(NT):
        dma("sync", xt(r), x_d[r * 128:(r + 1) * 128, :], s_syn)
    dma("sync", identI, ident_d[:, :], s_syn)
    dma("sync", glam1, glamask_d[:, :], s_syn)
    x_loaded = s_syn.n

    def wl0(k):
        a = Ch if k < 8 else Dh
        kk = k % 8
        return a[:, kk * 2048:(kk + 1) * 2048]

    def phase_l0ffn(tag):
        wait("scalar", s_vec)
        wait("gpsimd", s_vec)
        wait("gpsimd", s_ten)
        est = [E[:, 0:1024], E[:, 1024:2048]]
        wait("sync", s_vec)
        wait("gpsimd", s_vec)
        wait("vector", s_ten)
        for kh in range(32):
            k, h = kh // 2, kh % 2
            tok = wst.load([(est[wst.i % 2],
                             l0wT_d[k * 128:(k + 1) * 128,
                                    h * 1024:(h + 1) * 1024])])
            wst.wait_tok("vector", tok)
            op("vector", "tensor_copy", sem=s_vec,
               out=wl0(k)[:, h * 1024:(h + 1) * 1024], in_=est[tok[0]])
            wst.mark_free(tok, s_vec)
        w_ready = (s_vec.n, s_syn.n)

        b.aux_dma("gpsimd", s_gpd, S1[0:1, 0:1], l0ws_d[0:1, 0:1])
        b.aux_dma("gpsimd", s_gpd, S1[:, 0:1], S1[0:1, 0:1].broadcast_to((1, 128)))
        hb = A[:, 4096:6144]
        b.aux_dma("gpsimd", s_gpd, hb, l0bias_d[None, :].broadcast_to((128, D)))
        wait("vector", s_gpd)
        vts(S1[:, 1:2], S1[:, 0:1], 0.5, None, ALU.mult)
        vts(hb, hb, 0.5, None, ALU.mult)
        drain("vector")

        wait("scalar", s_syn, x_loaded)
        wait("scalar", s_vec)
        for r in range(NT):
            if r > 0:
                drain("scalar")
            act(A[:, 6144:8192], xt(r), AF.Square, accum=S1[:, 48 + r:49 + r],
                sem=s_sca)
        sq_done = s_sca.n
        wait("vector", s_syn, x_loaded)
        for r in range(NT):
            op("vector", "tensor_reduce", out=S1[:, 40:41], in_=xt(r),
               axis=AX.X, op=ALU.add)
            drain("vector")
            vts(S1[:, 16 + r:17 + r], S1[:, 40:41], 1.0 / D, None, ALU.mult)
            drain("vector")
            if r == 0:
                wait("vector", s_sca, sq_done)
            vtt(S1[:, 42:43], S1[:, 16 + r:17 + r], S1[:, 16 + r:17 + r], ALU.mult)
            drain("vector")
            vts(S1[:, 43:44], S1[:, 48 + r:49 + r], 1.0 / D, None, ALU.mult)
            drain("vector")
            vtt(S1[:, 42:43], S1[:, 43:44], S1[:, 42:43], ALU.subtract)
            drain("vector")
            rsqrt_small(S1[:, 24 + r:25 + r], S1[:, 42:43], 1, LN_EPS)
            drain("vector")
            vts(A[:, 0:2048], xt(r), S1[:, 16 + r:17 + r], None, ALU.subtract)
            drain("vector")
            op("vector", "tensor_reduce", out=S1[:, 44:45], in_=A[:, 0:2048],
               axis=AX.X, op=ALU.max, apply_absolute_value=True)
            drain("vector")
            vts(S1[:, 44:45], S1[:, 44:45], S1[:, 24 + r:25 + r], None, ALU.mult)
            drain("vector")
            if r == 0:
                vtt(S1[:, 8:9], S1[:, 44:45], S1[:, 44:45], ALU.max)
            else:
                vtt(S1[:, 8:9], S1[:, 8:9], S1[:, 44:45], ALU.max)
            drain("vector")
        op("vector", "tensor_copy", out=S1[:, 8:9], in_=S1[:, 8:9], sem=s_vec)
        stats_done = s_vec.n

        wait("gpsimd", s_vec, stats_done)
        b.aux_dma("gpsimd", s_gpd, S2[0:1, 0:128], S1[:, 8:9])
        wait("vector", s_gpd)
        op("vector", "tensor_reduce", out=S2[0:1, 128:129], in_=S2[0:1, 0:128],
           axis=AX.X, op=ALU.max, sem=s_vec)
        wait("gpsimd", s_vec)
        b.aux_dma("gpsimd", s_gpd, ccs_in.ap()[0:1, 0:1], S2[0:1, 128:129])
        raw("gpsimd", lambda e: e.collective_compute(
            "AllReduce", ALU.max, replica_groups=[list(range(NC))],
            ins=[ccs_in.ap()[0:1, 0:1].opt()],
            outs=[ccs_out.ap()[0:1, 0:1].opt()]).then_inc(s_cc.h, 1))
        s_cc.n += 1
        wait("gpsimd", s_cc)
        b.aux_dma("gpsimd", s_gpd, S2[:, 129:130],
                  ccs_out.ap()[0:1, 0:1].broadcast_to((1, 128)))
        wait("vector", s_gpd)
        vts(S2[:, 130:131], S2[:, 129:130], 1.0 / FP8_MAX, SCALE_EPS,
            ALU.mult, ALU.max)
        drain("vector")
        op("vector", "reciprocal", out=S2[:, 131:132], in_=S2[:, 130:131])
        drain("vector")

        for r in range(NT):
            vts(S1[:, 36:37], S1[:, 24 + r:25 + r], S2[:, 131:132], None,
                ALU.mult)
            drain("vector")
            vts(A[:, 0:2048], xt(r), S1[:, 16 + r:17 + r], None, ALU.subtract)
            drain("vector")
            vts(A[:, 0:2048], A[:, 0:2048], S1[:, 36:37], None, ALU.mult)
            drain("vector")
            ui = A[:, 0:2048].bitcast(I32)
            ti = A[:, 2048:4096].bitcast(I32)
            vts(ti, ui, 0x7FFFFFFF, 23, ALU.bitwise_and, ALU.logical_shift_right)
            drain("vector")
            vts(ti, ti, 121, 20, ALU.max, ALU.add)
            drain("vector")
            vts(ti, ti, 23, 0x00400000, ALU.logical_shift_left, ALU.bitwise_or)
            drain("vector")
            tif = A[:, 2048:4096]
            vtt(A[:, 0:2048], A[:, 0:2048], tif, ALU.add)
            drain("vector")
            xq = Ah[:, 12288:14336]
            vtt(xq, A[:, 0:2048], tif, ALU.subtract, sem=s_vec)
            xq_done = s_vec.n
            wait("tensor", s_vec, xq_done)
            for k in range(16):
                pc = (PT0, PT1)[k % 2]
                if k >= 2:
                    wait("tensor", s_vec)
                transpose(pc[:, 0:128], xq[:, k * 128:(k + 1) * 128], sem=s_ten)
                wait("vector", s_ten)
                op("vector", "tensor_copy", sem=s_vec,
                   out=Bh[:, k * 128:(k + 1) * 128], in_=pc[:, 0:128])
            wait("tensor", s_vec)
            if r == 0:
                wait("tensor", s_vec, w_ready[0])
            for rnd in range(2):
                if rnd == 1 or r > 0:
                    wait("tensor", s_vec)
                for k in range(16):
                    for fo in range(2):
                        o = rnd * 2 + fo
                        mm(PB[:, fo * 512:(fo + 1) * 512],
                           Bh[:, k * 128:(k + 1) * 128],
                           wl0(k)[:, o * 512:(o + 1) * 512],
                           start=(k == 0), stop=(k == 15),
                           sem=s_ten if (k == 15 and fo == 1) else None)
                wait("vector", s_ten)
                for fo in range(2):
                    o = rnd * 2 + fo
                    sl = slice(o * 512, (o + 1) * 512)
                    psl = slice(fo * 512, (fo + 1) * 512)
                    vstt(A[:, sl], PB[:, psl], S1[:, 1:2], hb[:, sl],
                         ALU.mult, ALU.add)
                drain("vector")
                for fo in range(2):
                    o = rnd * 2 + fo
                    sl = slice(o * 512, (o + 1) * 512)
                    vtt(xt(r)[:, sl], xt(r)[:, sl], A[:, sl], ALU.add,
                        sem=s_vec if fo == 1 else None)
                drain("vector")

    phase_l0ffn("p1")

    op("vector", "memset", ap=KON[:, 96:97], constant=1.0)
    drain("vector")

    def xlnT(k):
        return Ah[:, k * 1024:(k + 1) * 1024]

    def phase_gla():
        wait("scalar", s_vec)
        for r in range(NT):
            drain("scalar")
            act(A[:, 6144:8192], xt(r), AF.Square, accum=S1[:, 48 + r:49 + r],
                sem=s_sca)
        sq_done = s_sca.n
        for r in range(NT):
            op("vector", "tensor_reduce", out=S1[:, 40:41], in_=xt(r),
               axis=AX.X, op=ALU.add)
            drain("vector")
            vts(S1[:, 16 + r:17 + r], S1[:, 40:41], 1.0 / D, None, ALU.mult)
            drain("vector")
            if r == 0:
                wait("vector", s_sca, sq_done)
            vtt(S1[:, 42:43], S1[:, 16 + r:17 + r], S1[:, 16 + r:17 + r], ALU.mult)
            drain("vector")
            vts(S1[:, 43:44], S1[:, 48 + r:49 + r], 1.0 / D, None, ALU.mult)
            drain("vector")
            vtt(S1[:, 42:43], S1[:, 43:44], S1[:, 42:43], ALU.subtract)
            drain("vector")
            rsqrt_small(S1[:, 24 + r:25 + r], S1[:, 42:43], 1, LN_EPS)
            drain("vector")

        for j in range(8):
            b.aux_dma("gpsimd", s_gpd, S3[:, 288 + j:289 + j],
                      alphas_d[None, j:j + 1].broadcast_to((1, 128)))

        xlnf = E[:, 0:2048]
        xlh = Dh[:, 0:2048]
        for r in range(NT):
            vts(xlnf, xt(r), S1[:, 16 + r:17 + r], None, ALU.subtract)
            drain("vector")
            vts(xlnf, xlnf, S1[:, 24 + r:25 + r], None, ALU.mult)
            drain("vector")
            op("vector", "tensor_copy", out=xlh, in_=xlnf, sem=s_vec)
            wait("tensor", s_vec)
            for k in range(16):
                pc = (PT0, PT1)[k % 2]
                if k >= 2:
                    wait("tensor", s_vec)
                transpose(pc[:, 0:128], xlh[:, k * 128:(k + 1) * 128], sem=s_ten)
                wait("vector", s_ten)
                op("vector", "tensor_copy", sem=s_vec,
                   out=xlnT(k)[:, r * 128:(r + 1) * 128], in_=pc[:, 0:128])
            drain("vector")

        wstg = [E[:, 0:512], E[:, 512:1024]]
        wbf = [Eh[:, 3072:3584], Eh[:, 3584:4096]]
        wbf_free = {0: 0, 1: 0}

        def stream_wtile(w_d, k, c0, w):
            tok = wst.load([(wstg[wst.i % 2][:, 0:w],
                             w_d[k * 128:(k + 1) * 128, c0:c0 + w])])
            p = tok[0]
            wst.wait_tok("vector", tok)
            wait("vector", s_ten, wbf_free[p])
            op("vector", "tensor_copy", sem=s_vec, out=wbf[p][:, 0:w],
               in_=wstg[p][:, 0:w])
            wst.mark_free(tok, s_vec)
            return p, s_vec.n

        def wtile_done(p):
            wbf_free[p] = s_ten.n

        onesc = KON[:, 96:97]
        g_eza = Db[:, 2560:3072]
        g_dec = Db[:, 3072:3584]
        g_k = Db[:, 3584:4096]
        g_cp = Db[:, 4096:4608]
        g_rcp = Db[:, 4608:5120]
        paslot = [PA0[:, 0:512], PA0[:, 512:1024], PA1[:, 0:512], PA1[:, 512:1024]]

        wait("sync", s_vec)
        wait("gpsimd", s_vec)
        for og in range(8):
            wait("tensor", s_vec)
            for dm in range(16):
                p_, wcnt = stream_wtile(wk_d, dm, og * 256, 256)
                wb = wbf[p_]
                wait("tensor", s_vec, wcnt)
                for oo in range(2):
                    for th in range(2):
                        mm(paslot[oo * 2 + th],
                           wb[:, oo * 128:(oo + 1) * 128],
                           xlnT(dm)[:, th * 512:(th + 1) * 512],
                           start=(dm == 0), stop=(dm == 15),
                           sem=s_ten if (oo == 1 and th == 1) else None)
                wtile_done(p_)
            wait("scalar", s_ten)
            for oo in range(2):
                ot = og * 2 + oo
                for th in range(2):
                    ps_ = paslot[oo * 2 + th]
                    drain("scalar")
                    wait("scalar", s_vec)
                    act(g_eza, ps_, AF.Exp, sem=s_sca)
                    wait("vector", s_sca)
                    vts(g_eza, g_eza, 1.0, None, ALU.add)
                    drain("vector")
                    op("vector", "reciprocal", out=g_dec, in_=g_eza)
                    drain("vector")
                    vts(g_k, g_dec, 1.0, -1.0, ALU.subtract, ALU.mult)
                    drain("vector")
                    for c in range(8):
                        op("vector", "tensor_tensor_scan",
                           out=g_cp[:, c * 64:(c + 1) * 64],
                           data0=g_dec[:, c * 64:(c + 1) * 64],
                           data1=onesc.broadcast_to((128, 64)),
                           initial=1.0, op0=ALU.mult, op1=ALU.mult)
                    drain("vector")
                    op("vector", "tensor_copy",
                       out=Bh[:, ot * 1024 + th * 512: ot * 1024 + (th + 1) * 512],
                       in_=g_cp)
                    op("vector", "tensor_copy",
                       out=S3[:, ot * 16 + th * 8: ot * 16 + (th + 1) * 8]
                           .rearrange("p (c o) -> p c o", o=1),
                       in_=g_cp.rearrange("p (c s) -> p c s", s=64)[:, :, 63:64])
                    drain("vector")
                    op("vector", "reciprocal", out=g_rcp, in_=g_cp)
                    drain("vector")
                    vtt(Ch[:, ot * 1024 + th * 512: ot * 1024 + (th + 1) * 512],
                        g_k, g_rcp, ALU.mult, sem=s_vec)
                    drain("vector")

        for rg in range(2):
            for dv in range(4):
                wait("tensor", s_vec)
                for dm in range(16):
                    p_, wcnt = stream_wtile(wi_d, dm, dv * 512, 512)
                    wb = wbf[p_]
                    wait("tensor", s_vec, wcnt)
                    for rr in range(4):
                        r = rg * 4 + rr
                        mm(paslot[rr],
                           xlnT(dm)[:, r * 128:(r + 1) * 128],
                           wb[:, 0:512],
                           start=(dm == 0), stop=(dm == 15),
                           sem=s_ten if rr == 3 else None)
                    wtile_done(p_)
                wait("vector", s_ten)
                for rr in range(4):
                    r = rg * 4 + rr
                    op("vector", "tensor_copy", sem=s_vec,
                       out=Dh[:, r * 2048 + dv * 512: r * 2048 + (dv + 1) * 512],
                       in_=paslot[rr])
                drain("vector")

        cpc3 = S3[:, 0:256].rearrange("p (h i) -> p h i", i=16)
        sview = A[:, 0:2048].rearrange("p (h v) -> p h v", v=128)
        eview = E[:, 0:2048].rearrange("p (h v) -> p h v", v=128)
        k2tmp = KONh[:, 192:256]
        op("vector", "memset", ap=E[:, 0:2048], constant=0.0, sem=s_vec)
        drain("vector")

        def state_chunk(i, dst_f32, dview, into_sem):
            ob = 64 * (i % 2)
            rr = i // 2
            for h in range(16):
                vts(k2tmp, Ch[:, h * 1024 + i * 64: h * 1024 + (i + 1) * 64],
                    S3[:, h * 16 + i: h * 16 + i + 1], None, ALU.mult, sem=s_vec)
                drain("vector")
                wait("tensor", s_vec)
                transpose(PT0[ob:ob + 64, 0:128], k2tmp, sem=s_ten)
                wait("vector", s_ten)
                op("vector", "tensor_copy", sem=s_vec,
                   out=KONh[ob:ob + 64, 256:384], in_=PT0[ob:ob + 64, 0:128])
                drain("vector")
                wait("tensor", s_vec)
                pa_ = PA0 if h < 8 else PA1
                mm(pa_[:, (h % 8) * 128:(h % 8 + 1) * 128],
                   KONh[ob:ob + 64, 256:384],
                   Dh[ob:ob + 64,
                      rr * 2048 + h * 128: rr * 2048 + (h + 1) * 128],
                   start=True, stop=True,
                   sem=s_ten if h == 15 else None)
            wait("vector", s_ten)
            vtt(dview, dview,
                cpc3[:, :, i:i + 1].broadcast_to((128, 16, 128)), ALU.mult)
            drain("vector")
            vtt(dst_f32[:, 0:1024], dst_f32[:, 0:1024], PA0[:, 0:1024], ALU.add)
            drain("vector")
            vtt(dst_f32[:, 1024:2048], dst_f32[:, 1024:2048], PA1[:, 0:1024],
                ALU.add, sem=into_sem)
            drain("vector")

        for i in range(NCH):
            state_chunk(i, E[:, 0:2048], eview, s_vec)

        lam3 = S3[:, 256:272].rearrange("p (h o) -> p h o", o=1)
        op("vector", "tensor_copy", out=lam3, in_=cpc3[:, :, 0:1])
        drain("vector")
        for i in range(1, 16):
            vtt(lam3, lam3, cpc3[:, :, i:i + 1], ALU.mult)
            drain("vector")
        op("vector", "tensor_copy", out=S3[:, 272:273], in_=S3[:, 256:257],
           sem=s_vec)
        wait("gpsimd", s_vec)
        b.aux_dma("gpsimd", s_gpd, ccg_in.ap()[:, 0:16], S3[:, 256:272])
        b.aux_dma("gpsimd", s_gpd, ccg_in.ap()[:, 16:16 + D], E[:, 0:2048])
        pack_cnt = s_gpd.n
        raw("gpsimd", lambda e: e.collective_compute(
            "AllGather", ALU.bypass, replica_groups=[list(range(NC))],
            ins=[ccg_in.ap()[:, :].opt()],
            outs=[ccg_out.ap()[:, :].opt()]).then_inc(s_cc.h, 1))
        s_cc.n += 1

        wait("sync", s_gpd, pack_cnt)
        wait("gpsimd", s_gpd, pack_cnt)
        qtmp = E[:, 1024:1536]
        for og in range(8):
            wait("tensor", s_vec)
            for dm in range(16):
                p_, wcnt = stream_wtile(wq_d, dm, og * 256, 256)
                wb = wbf[p_]
                wait("tensor", s_vec, wcnt)
                for oo in range(2):
                    for th in range(2):
                        mm(paslot[oo * 2 + th],
                           wb[:, oo * 128:(oo + 1) * 128],
                           xlnT(dm)[:, th * 512:(th + 1) * 512],
                           start=(dm == 0), stop=(dm == 15),
                           sem=s_ten if (oo == 1 and th == 1) else None)
                wtile_done(p_)
            wait("scalar", s_ten)
            for oo in range(2):
                ot = og * 2 + oo
                for th in range(2):
                    drain("scalar")
                    wait("scalar", s_vec)
                    act(qtmp, paslot[oo * 2 + th], AF.Relu,
                        scale=float(DK) ** -0.5, sem=s_sca)
                    wait("vector", s_sca)
                    sl = slice(ot * 1024 + th * 512, ot * 1024 + (th + 1) * 512)
                    vtt(Bh[:, sl], qtmp, Bh[:, sl], ALU.mult, sem=s_vec)
                    drain("vector")

        wait("gpsimd", s_cc)
        wait("gpsimd", s_ten)
        wait("gpsimd", s_vec)
        op("vector", "memset", ap=A[:, 0:2048], constant=0.0)
        op("vector", "memset", ap=S3[:, 304:320], constant=1.0)
        vts(S3[:, 296:304], S3[:, 288:296], -1.0, 1.0, ALU.mult, ALU.add)
        drain("vector")
        rview = S3[:, 304:320].rearrange("p (h o) -> p h o", o=1)
        for j in range(6, -1, -1):
            wait("gpsimd", s_vec)
            b.aux_dma("gpsimd", s_gpd, A[:, 2048:2048 + 2064],
                      ccg_out.ap()[j * 128:(j + 1) * 128, :])
            wait("vector", s_gpd)
            sj = A[:, 2064:2064 + 2048].rearrange("p (h v) -> p h v", v=128)
            t3 = A[:, 4112:4112 + 2048].rearrange("p (h v) -> p h v", v=128)
            vtt(t3, sj, rview.broadcast_to((128, 16, 128)), ALU.mult)
            drain("vector")
            vstt(A[:, 0:2048], A[:, 4112:4112 + 2048], S3[:, 288 + j:289 + j],
                 A[:, 0:2048], ALU.mult, ALU.add)
            drain("vector")
            lamj = A[:, 2048:2064]
            vts(S3[:, 320:336], lamj, S3[:, 288 + j:289 + j], None, ALU.mult)
            drain("vector")
            vts(S3[:, 320:336], S3[:, 320:336], S3[:, 296 + j:297 + j], None,
                ALU.add)
            drain("vector")
            vtt(S3[:, 304:320], S3[:, 304:320], S3[:, 320:336], ALU.mult,
                sem=s_vec)
            drain("vector")
        wait("gpsimd", s_vec)
        b.aux_dma("gpsimd", s_gpd, A[:, 5120:7168],
                  gw_d[None, :].broadcast_to((128, D)))
        wait("vector", s_gpd)

        g_rep = A[:, 5120:7168]
        ost = A[:, 3072:5120]
        sbf = Ah[:, 4096:6144]
        for i in range(NCH):
            ob = 64 * (i % 2)
            rr = i // 2
            op("vector", "tensor_copy", out=sbf, in_=A[:, 0:2048], sem=s_vec)
            drain("vector")
            for hh in range(2):
                wait("tensor", s_vec)
                for ho in range(8):
                    h = hh * 8 + ho
                    mm(PA0[ob:ob + 64, ho * 64:(ho + 1) * 64],
                       Ch[:, h * 1024 + i * 64: h * 1024 + (i + 1) * 64],
                       Bh[:, h * 1024 + i * 64: h * 1024 + (i + 1) * 64],
                       start=True, stop=True, sem=s_ten if ho == 7 else None)
                wait("vector", s_ten)
                msb = Eh[ob:ob + 64, 0:512]
                vtt(msb.rearrange("p (c s) -> p c s", s=64),
                    PA0[ob:ob + 64, 0:512].rearrange("p (c s) -> p c s", s=64),
                    glam1.rearrange("s (o t) -> s o t", o=1)
                        .broadcast_to((64, 8, 64)),
                    ALU.mult, sem=s_vec)
                drain("vector")
                wait("tensor", s_vec)
                for ho in range(8):
                    h = hh * 8 + ho
                    mm(PB[ob:ob + 64, ho * 128:(ho + 1) * 128],
                       msb[:, ho * 64:(ho + 1) * 64],
                       Dh[ob:ob + 64,
                          rr * 2048 + h * 128: rr * 2048 + (h + 1) * 128],
                       start=True, stop=False)
                    mm(PB[ob:ob + 64, ho * 128:(ho + 1) * 128],
                       Bh[:, h * 1024 + i * 64: h * 1024 + (i + 1) * 64],
                       sbf[:, h * 128:(h + 1) * 128],
                       start=False, stop=True, sem=s_ten if ho == 7 else None)
                wait("vector", s_ten)
                op("vector", "tensor_copy", sem=s_vec,
                   out=ost[ob:ob + 64, hh * 1024:(hh + 1) * 1024],
                   in_=PB[ob:ob + 64, 0:1024])
                drain("vector")
            state_chunk(i, A[:, 0:2048], sview, s_vec)
            if i % 2 == 1:
                p = i // 2
                drain("scalar")
                wait("scalar", s_vec)
                act(E[:, 1024:2048], ost[:, 0:1024], AF.Square,
                    accum=S2[:, 350:351], sem=s_sca)
                drain("scalar")
                act(E[:, 1024:2048], ost[:, 1024:2048], AF.Square,
                    accum=S2[:, 351:352], sem=s_sca)
                wait("vector", s_sca)
                vtt(S2[:, 352:353], S2[:, 350:351], S2[:, 351:352], ALU.add)
                drain("vector")
                vts(S2[:, 352:353], S2[:, 352:353], 1.0 / D, None, ALU.mult)
                drain("vector")
                rsqrt_small(S2[:, 353:354], S2[:, 352:353], 1, LN_EPS)
                drain("vector")
                vts(ost, ost, S2[:, 353:354], None, ALU.mult)
                drain("vector")
                onh = Eh[:, 2048:4096]
                vtt(onh, ost, g_rep, ALU.mult, sem=s_vec)
                drain("vector")
                wait("tensor", s_vec)
                for k in range(16):
                    pc = (PT0, PT1)[k % 2]
                    if k >= 2:
                        wait("tensor", s_vec)
                    transpose(pc[:, 0:128], onh[:, k * 128:(k + 1) * 128],
                              sem=s_ten)
                    wait("vector", s_ten)
                    op("vector", "tensor_copy", sem=s_vec,
                       out=Bh[:, k * 1024 + p * 128: k * 1024 + (p + 1) * 128],
                       in_=pc[:, 0:128])
                drain("vector")

        wait("sync", s_vec)
        wait("gpsimd", s_vec)
        wait("vector", s_ten)
        for kq in range(64):
            k, qq = kq // 4, kq % 4
            tok = wst.load([(wstg[wst.i % 2],
                             wo_d[k * 128:(k + 1) * 128,
                                  qq * 512:(qq + 1) * 512])])
            p = tok[0]
            wst.wait_tok("vector", tok)
            op("vector", "tensor_copy", sem=s_vec,
               out=wl0(k)[:, qq * 512:(qq + 1) * 512], in_=wstg[p])
            wst.mark_free(tok, s_vec)
        for r in range(NT):
            for rnd in range(2):
                wait("tensor", s_vec)
                for k in range(16):
                    for fo in range(2):
                        o = rnd * 2 + fo
                        mm(PB[:, fo * 512:(fo + 1) * 512],
                           Bh[:, k * 1024 + r * 128: k * 1024 + (r + 1) * 128],
                           wl0(k)[:, o * 512:(o + 1) * 512],
                           start=(k == 0), stop=(k == 15),
                           sem=s_ten if (k == 15 and fo == 1) else None)
                wait("vector", s_ten)
                for fo in range(2):
                    o = rnd * 2 + fo
                    sl = slice(o * 512, (o + 1) * 512)
                    vtt(xt(r)[:, sl], xt(r)[:, sl], PB[:, fo * 512:(fo + 1) * 512],
                        ALU.add, sem=s_vec if fo == 1 else None)
                drain("vector")

    if NSUB >= 2:
        phase_gla()


    def l1_presum():
        wpre = DStream(b, "wpre", 2, queue="scalar")
        pstg = [B[:, 4096:4608], B[:, 4608:5120]]
        junk = B[:, 5120:5632]
        accp = [B[:, 5640:5641], B[:, 5641:5642]]
        runc = [B[:, 5644:5645], B[:, 5645:5646], B[:, 5646:5647]]
        op("vector", "memset", ap=B[:, 5644:5647], constant=0.0, sem=s_vec)
        drain("vector")
        wait("scalar", s_ten)
        acc_gate = [0, 0]
        for widx, w_d in ((0, w1T_d), (1, w2T_d), (2, w3T_d)):
            ncols = 16 if widx < 2 else 4
            for t in range(256):
                r, c = divmod(t, ncols)
                tok = wpre.load([(pstg[wpre.i % 2],
                                  w_d[r * 128:(r + 1) * 128,
                                      c * 512:(c + 1) * 512])])
                p = tok[0]
                wpre.wait_tok("scalar", tok)
                drain("scalar")
                wait("scalar", s_vec, acc_gate[p])
                act(junk, pstg[p], AF.Abs, accum=accp[p], sem=s_sca)
                wpre.mark_free(tok, s_sca)
                wait("vector", s_sca)
                vtt(runc[widx], runc[widx], accp[p], ALU.add, sem=s_vec)
                drain("vector")
                acc_gate[p] = s_vec.n
        wait("gpsimd", s_vec)
        for widx in range(3):
            b.aux_dma("gpsimd", s_gpd, B[0:1, 5660:5788], runc[widx])
            wait("vector", s_gpd)
            op("vector", "tensor_reduce", out=B[0:1, 5800 + widx:5801 + widx],
               in_=B[0:1, 5660:5788], axis=AX.X, op=ALU.add, sem=s_vec)
            drain("vector")
            wait("gpsimd", s_vec)
            b.aux_dma("gpsimd", s_gpd, S3[:, 336 + widx:337 + widx],
                      B[0:1, 5800 + widx:5801 + widx].broadcast_to((1, 128)))
        wait("vector", s_gpd)
        for widx in range(3):
            vts(S3[:, 336 + widx:337 + widx], S3[:, 336 + widx:337 + widx],
                float(2.0 ** -24), None, ALU.mult)
        drain("vector")
        vts(S3[:, 339:340], S3[:, 338:339], 0.5, None, ALU.mult, sem=s_vec)
        drain("vector")

    def phase_l1ffn(first):
        wait("scalar", s_vec)
        wait("gpsimd", s_vec)
        wait("gpsimd", s_ten)
        for r in range(NT):
            drain("scalar")
            wait("scalar", s_vec)
            act(A[:, 6144:8192], xt(r), AF.Square, accum=S1[:, 48 + r:49 + r],
                sem=s_sca)
        sq_done = s_sca.n
        for r in range(NT):
            op("vector", "tensor_reduce", out=S1[:, 40:41], in_=xt(r),
               axis=AX.X, op=ALU.add)
            drain("vector")
            vts(S1[:, 16 + r:17 + r], S1[:, 40:41], 1.0 / D, None, ALU.mult)
            drain("vector")
            if r == 0:
                wait("vector", s_sca, sq_done)
            vtt(S1[:, 42:43], S1[:, 16 + r:17 + r], S1[:, 16 + r:17 + r], ALU.mult)
            drain("vector")
            vts(S1[:, 43:44], S1[:, 48 + r:49 + r], 1.0 / D, None, ALU.mult)
            drain("vector")
            vtt(S1[:, 42:43], S1[:, 43:44], S1[:, 42:43], ALU.subtract)
            drain("vector")
            rsqrt_small(S1[:, 24 + r:25 + r], S1[:, 42:43], 1, LN_EPS)
            drain("vector")

        def quant_tile(dst_b16, src_f32, W):
            ng = W // 64
            s3v = src_f32.rearrange("p (g k) -> p g k", k=64)
            d3v = dst_b16.rearrange("p (g k) -> p g k", k=64)
            op("vector", "tensor_reduce", out=S2[:, 0:ng], in_=s3v,
               axis=AX.X, op=ALU.max, apply_absolute_value=True)
            drain("vector")
            vts(S2[:, 0:ng], S2[:, 0:ng], GEPS, None, ALU.max)
            drain("vector")
            op("vector", "reciprocal", out=S2[:, 128:128 + ng], in_=S2[:, 0:ng])
            drain("vector")
            vts(S2[:, 128:128 + ng], S2[:, 128:128 + ng], QB, None, ALU.mult)
            drain("vector")
            vts(S2[:, 0:ng], S2[:, 0:ng], 1.0 / QB, None, ALU.mult)
            drain("vector")
            sc3a = S2[:, 128:128 + ng].rearrange("p (g o) -> p g o", o=1)
            sc3b = S2[:, 0:ng].rearrange("p (g o) -> p g o", o=1)
            vtt(s3v, s3v, sc3a.broadcast_to((128, ng, 64)), ALU.mult)
            drain("vector")
            vts(src_f32, src_f32, RNE_C, RNE_C, ALU.add, ALU.subtract)
            drain("vector")
            vtt(d3v, s3v, sc3b.broadcast_to((128, ng, 64)), ALU.mult, sem=s_vec)
            drain("vector")

        op("vector", "memset", ap=S2[:, 200:201], constant=0.0)
        drain("vector")
        xqh = Dh[:, 0:2048]
        for r in range(NT):
            vts(A[:, 0:2048], xt(r), S1[:, 16 + r:17 + r], None, ALU.subtract)
            drain("vector")
            vts(A[:, 0:2048], A[:, 0:2048], S1[:, 24 + r:25 + r], None, ALU.mult)
            drain("vector")
            quant_tile(xqh, A[:, 0:2048], D)
            wait("tensor", s_vec)
            for k in range(16):
                pc = (PT0, PT1)[k % 2]
                if k >= 2:
                    wait("tensor", s_vec)
                transpose(pc[:, 0:128], xqh[:, k * 128:(k + 1) * 128], sem=s_ten)
                wait("vector", s_ten)
                op("vector", "tensor_copy", sem=s_vec,
                   out=Bh[:, k * 1024 + r * 128: k * 1024 + (r + 1) * 128],
                   in_=pc[:, 0:128])
            drain("vector")

        wstg = [E[:, 0:512], E[:, 512:1024]]
        wbf = [Eh[:, 3072:3584], Eh[:, 3584:4096]]
        l1_free = {0: 0, 1: 0}

        def l1_wtile(w_d, r0, c0):
            tok = wst.load([(wstg[wst.i % 2],
                             w_d[r0:r0 + 128, c0:c0 + 512])])
            p = tok[0]
            wst.wait_tok("scalar", tok)
            drain("scalar")
            wait("scalar", s_ten, l1_free[p])
            act(wbf[p], wstg[p], AF.Sign, sem=s_sca)
            wst.mark_free(tok, s_sca)
            return p, s_sca.n

        paslot2 = [PA0[:, 0:512], PA0[:, 512:1024], PA1[:, 0:512],
                   PA1[:, 512:1024]]
        y2a = [A[:, 0:2048], A[:, 2048:4096], A[:, 4096:6144], A[:, 6144:8192]]
        wait("sync", s_vec)
        wait("sync", s_ten)
        wait("gpsimd", s_vec)
        wait("gpsimd", s_ten)
        for rg in range(2):
            for q in range(4):
                for iw, w_d in ((2, w2T_d), (1, w1T_d)):
                    for dffc in range(4):
                        wait("tensor", s_vec)
                        wait("tensor", s_sca)
                        c0 = q * 2048 + dffc * 512
                        for dm in range(16):
                            p_, wcnt = l1_wtile(w_d, dm * 128, c0)
                            wait("tensor", s_sca, wcnt)
                            for tt in range(4):
                                r = rg * 4 + tt
                                mm(paslot2[tt],
                                   Bh[:, dm * 1024 + r * 128:
                                      dm * 1024 + (r + 1) * 128],
                                   wbf[p_],
                                   start=(dm == 0), stop=(dm == 15),
                                   sem=s_ten if tt == 3 else None)
                            l1_free[p_] = s_ten.n
                        sl = slice(dffc * 512, (dffc + 1) * 512)
                        if iw == 2:
                            wait("scalar", s_ten)
                            for tt in range(4):
                                drain("scalar")
                                wait("scalar", s_vec)
                                act(y2a[tt][:, sl], paslot2[tt], AF.Copy,
                                    scale=S3[:, 337:338], sem=s_sca)
                            wait("vector", s_sca)
                            op("vector", "tensor_copy", out=S2[:, 200:201],
                               in_=S2[:, 200:201], sem=s_vec)
                        else:
                            wait("scalar", s_ten)
                            for tt in range(4):
                                drain("scalar")
                                wait("scalar", s_vec)
                                act(E[:, 1024:1536], paslot2[tt], AF.Silu,
                                    scale=S3[:, 336:337], sem=s_sca)
                                wait("vector", s_sca)
                                vtt(y2a[tt][:, sl], E[:, 1024:1536],
                                    y2a[tt][:, sl], ALU.mult, sem=s_vec)
                                drain("vector")
                for tt in range(4):
                    hq = Dh[:, 8192 + (tt % 2) * 2048: 8192 + (tt % 2 + 1) * 2048]
                    quant_tile(hq, y2a[tt], 2048)
                    wait("tensor", s_vec)
                    for k in range(16):
                        pc = (PT0, PT1)[k % 2]
                        if k >= 2:
                            wait("tensor", s_vec)
                        transpose(pc[:, 0:128], hq[:, k * 128:(k + 1) * 128],
                                  sem=s_ten)
                        wait("vector", s_ten)
                        op("vector", "tensor_copy", sem=s_vec,
                           out=Dh[:, k * 512 + tt * 128: k * 512 + (tt + 1) * 128],
                           in_=pc[:, 0:128])
                    drain("vector")
                for fo in range(4):
                    wait("tensor", s_vec)
                    for dffc in range(16):
                        p_, wcnt = l1_wtile(w3T_d, q * 2048 + dffc * 128,
                                            fo * 512)
                        wait("tensor", s_sca, wcnt)
                        for tt in range(4):
                            mm(paslot2[tt],
                               Dh[:, dffc * 512 + tt * 128:
                                  dffc * 512 + (tt + 1) * 128],
                               wbf[p_],
                               start=(dffc == 0), stop=(dffc == 15),
                               sem=s_ten if tt == 3 else None)
                        l1_free[p_] = s_ten.n
                    wait("vector", s_ten)
                    for tt in range(4):
                        zsl = Cb[:, tt * 2048 + fo * 512:
                                 tt * 2048 + (fo + 1) * 512]
                        if q == 0:
                            op("vector", "tensor_copy", out=zsl,
                               in_=paslot2[tt],
                               sem=s_vec if tt == 3 else None)
                        else:
                            vtt(zsl, zsl, paslot2[tt], ALU.add,
                                sem=s_vec if tt == 3 else None)
                    drain("vector")
            for tt in range(4):
                r = rg * 4 + tt
                vstt(xt(r), Cb[:, tt * 2048:(tt + 1) * 2048], S3[:, 339:340],
                     xt(r), ALU.mult, ALU.add, sem=s_vec)
            drain("vector")

    scr_hk = nc.dram_tensor("scr_hk", [D, T], BF16)
    scr_hv = nc.dram_tensor("scr_hv", [T, D], BF16)

    def phase_swa():
        wait("scalar", s_vec)
        wait("gpsimd", s_vec)
        wait("gpsimd", s_ten)
        for r in range(NT):
            drain("scalar")
            wait("scalar", s_vec)
            act(A[:, 6144:8192], xt(r), AF.Square, accum=S1[:, 48 + r:49 + r],
                sem=s_sca)
        sq_done = s_sca.n
        for r in range(NT):
            op("vector", "tensor_reduce", out=S1[:, 40:41], in_=xt(r),
               axis=AX.X, op=ALU.add)
            drain("vector")
            vts(S1[:, 16 + r:17 + r], S1[:, 40:41], 1.0 / D, None, ALU.mult)
            drain("vector")
            if r == 0:
                wait("vector", s_sca, sq_done)
            vtt(S1[:, 42:43], S1[:, 16 + r:17 + r], S1[:, 16 + r:17 + r], ALU.mult)
            drain("vector")
            vts(S1[:, 43:44], S1[:, 48 + r:49 + r], 1.0 / D, None, ALU.mult)
            drain("vector")
            vtt(S1[:, 42:43], S1[:, 43:44], S1[:, 42:43], ALU.subtract)
            drain("vector")
            rsqrt_small(S1[:, 24 + r:25 + r], S1[:, 42:43], 1, LN_EPS)
            drain("vector")

        b.aux_dma("gpsimd", s_gpd, S2[0:64, 370:371], iota64_d[0:64, 0:1])
        b.aux_dma("gpsimd", s_gpd, A[0:64, 2048:4096],
                  pos_d[None, :].broadcast_to((64, 2 * T)))
        wait("scalar", s_gpd)
        drain("scalar")
        wait("scalar", s_vec)
        act(S2[0:64, 371:372], S2[0:64, 370:371], AF.Exp,
            scale=-float(np.log(10000.0) / 64.0), sem=s_sca)
        wait("vector", s_sca)
        FQ = A[0:64, 2048:4096]
        TK = A[0:64, 4096:6144]
        R1 = A[0:64, 6144:8192]
        cosb = Ah[0:64, 0:2048]
        sinb = Ah[0:64, 2048:4096]
        C1 = 6.28125
        C2 = float(2.0 * np.pi - 6.28125)
        vts(FQ, FQ, S2[0:64, 371:372], None, ALU.mult)
        drain("vector")
        for trig in range(2):
            if trig == 1:
                vts(FQ, FQ, float(np.pi / 2.0), None, ALU.add)
                drain("vector")
            vts(TK, FQ, float(1.0 / (2.0 * np.pi)), None, ALU.mult)
            drain("vector")
            vts(TK, TK, RNE_C, RNE_C, ALU.add, ALU.subtract)
            drain("vector")
            vstt(R1, TK, -C1, FQ, ALU.mult, ALU.add)
            drain("vector")
            vstt(R1, TK, -C2, R1, ALU.mult, ALU.add)
            drain("vector")
            vts(R1, R1, 3.1415925, -3.1415925, ALU.min, ALU.max, sem=s_vec)
            drain("vector")
            drain("scalar")
            wait("scalar", s_vec)
            act(TK, R1, AF.Sin, sem=s_sca)
            wait("vector", s_sca)
            op("vector", "tensor_copy", out=(sinb if trig == 0 else cosb),
               in_=TK, sem=s_vec)
            drain("vector")
        cosD = Ah[:, 4096:6144]
        sinS = Ah[:, 6144:8192]
        op("vector", "tensor_copy", out=cosD[0:64, :], in_=cosb)
        vts(sinS[0:64, :], sinb, -1.0, None, ALU.mult, sem=s_vec)
        drain("vector")
        wait("gpsimd", s_vec)
        b.aux_dma("gpsimd", s_gpd, cosD[64:128, :], cosb)
        b.aux_dma("gpsimd", s_gpd, sinS[64:128, :], sinb)
        wait("vector", s_gpd)

        xlnf = E[:, 0:2048]
        xlh = Dh[:, 0:2048]
        wait("vector", s_ten)
        for r in range(NT):
            vts(xlnf, xt(r), S1[:, 16 + r:17 + r], None, ALU.subtract)
            drain("vector")
            vts(xlnf, xlnf, S1[:, 24 + r:25 + r], None, ALU.mult)
            drain("vector")
            op("vector", "tensor_copy", out=xlh, in_=xlnf, sem=s_vec)
            wait("tensor", s_vec)
            for k in range(16):
                pc = (PT0, PT1)[k % 2]
                if k >= 2:
                    wait("tensor", s_vec)
                transpose(pc[:, 0:128], xlh[:, k * 128:(k + 1) * 128], sem=s_ten)
                wait("vector", s_ten)
                op("vector", "tensor_copy", sem=s_vec,
                   out=Bh[:, k * 1024 + r * 128: k * 1024 + (r + 1) * 128],
                   in_=pc[:, 0:128])
            drain("vector")

        def xlnTb(k):
            return Bh[:, k * 1024:(k + 1) * 1024]

        wstg = [E[:, 0:512], E[:, 512:1024]]
        wbf = [Eh[:, 3072:3584], Eh[:, 3584:4096]]
        wbf_free = {0: 0, 1: 0}

        def stream_wtile(w_d, k, c0, w):
            tok = wst.load([(wstg[wst.i % 2][:, 0:w],
                             w_d[k * 128:(k + 1) * 128, c0:c0 + w])])
            p = tok[0]
            wst.wait_tok("vector", tok)
            wait("vector", s_ten, wbf_free[p])
            op("vector", "tensor_copy", sem=s_vec, out=wbf[p][:, 0:w],
               in_=wstg[p][:, 0:w])
            wst.mark_free(tok, s_vec)
            return p, s_vec.n

        paslot = [PA0[:, 0:512], PA0[:, 512:1024], PA1[:, 0:512], PA1[:, 512:1024]]

        rot_t = Ah[:, 8192:9216]
        tmp_t = Ah[:, 9216:10240]

        def rope(tile):
            op("vector", "tensor_copy", out=tile[0:64, :], in_=tile[0:64, :],
               sem=s_vec)
            drain("vector")
            wait("gpsimd", s_vec)
            b.aux_dma("gpsimd", s_gpd, rot_t[0:64, :], tile[64:128, :])
            b.aux_dma("gpsimd", s_gpd, rot_t[64:128, :], tile[0:64, :])
            wait("vector", s_gpd)
            vtt(tmp_t, rot_t, sinS[:, 1024:2048], ALU.mult)
            drain("vector")
            vtt(tile, tile, cosD[:, 1024:2048], ALU.mult)
            drain("vector")
            vtt(tile, tile, tmp_t, ALU.add, sem=s_vec)
            drain("vector")

        qh = Ah[:, 12288:14336]
        wait("sync", s_vec)
        wait("gpsimd", s_vec)
        for wqk in range(2):
            w_d = awk_d if wqk == 0 else awq_d
            for og in range(8):
                wait("tensor", s_vec)
                for dm in range(16):
                    p_, wcnt = stream_wtile(w_d, dm, og * 256, 256)
                    wait("tensor", s_vec, wcnt)
                    for oo in range(2):
                        for th in range(2):
                            mm(paslot[oo * 2 + th],
                               wbf[p_][:, oo * 128:(oo + 1) * 128],
                               xlnTb(dm)[:, th * 512:(th + 1) * 512],
                               start=(dm == 0), stop=(dm == 15),
                               sem=s_ten if (oo == 1 and th == 1) else None)
                    wbf_free[p_] = s_ten.n
                wait("vector", s_ten)
                wait("vector", s_gpd)
                for oo in range(2):
                    h = og * 2 + oo
                    hq_t = qh[:, (oo % 2) * 1024:((oo % 2) + 1) * 1024]
                    for th in range(2):
                        op("vector", "tensor_copy",
                           out=hq_t[:, th * 512:(th + 1) * 512],
                           in_=paslot[oo * 2 + th])
                    drain("vector")
                    rope(hq_t)
                    if wqk == 0:
                        op("vector", "tensor_copy", sem=s_vec,
                           out=Ch[:, h * 1024:(h + 1) * 1024], in_=hq_t)
                        drain("vector")
                    else:
                        wait("gpsimd", s_vec)
                        b.aux_dma("gpsimd", s_gpd,
                                  scr_q.ap()[h * 128:(h + 1) * 128, :], hq_t)

        wait("gpsimd", s_vec)
        for h in range(16):
            b.aux_dma("gpsimd", s_gpd, cck_in.ap()[h * 128:(h + 1) * 128, :],
                      Ch[:, h * 1024:(h + 1) * 1024])

        for rg in range(2):
            for dv in range(4):
                wait("tensor", s_vec)
                for dm in range(16):
                    p_, wcnt = stream_wtile(awv_d, dm, dv * 512, 512)
                    wait("tensor", s_vec, wcnt)
                    for rr in range(4):
                        r = rg * 4 + rr
                        mm(paslot[rr],
                           xlnTb(dm)[:, r * 128:(r + 1) * 128],
                           wbf[p_][:, 0:512],
                           start=(dm == 0), stop=(dm == 15),
                           sem=s_ten if rr == 3 else None)
                    wbf_free[p_] = s_ten.n
                wait("vector", s_ten)
                for rr in range(4):
                    r = rg * 4 + rr
                    op("vector", "tensor_copy", sem=s_vec,
                       out=Dh[:, r * 2048 + dv * 512: r * 2048 + (dv + 1) * 512],
                       in_=paslot[rr])
                drain("vector")
        wait("gpsimd", s_vec)
        for r in range(NT):
            b.aux_dma("gpsimd", s_gpd, ccv_in.ap()[r * 128:(r + 1) * 128, :],
                      Dh[:, r * 2048:(r + 1) * 2048])

        for gi, go in ((cck_in, cck_outF), (ccv_in, ccv_outF)):
            raw("gpsimd", lambda e, gi=gi, go=go: e.collective_compute(
                "AllGather", ALU.bypass, replica_groups=[list(range(NC))],
                ins=[gi.ap()[:, :].opt()],
                outs=[go.ap()[:, :].opt()]).then_inc(s_cc.h, 1))
            s_cc.n += 1
        wait("gpsimd", s_cc)
        for j in range(8):
            b.aux_dma("gpsimd", s_gpd, S3[:, 344 + j:345 + j],
                      nbr_d[None, j:j + 1].broadcast_to((1, 128)))
        stg = [Ah[:, j * 1024:(j + 1) * 1024] for j in range(8)]
        accb = Ah[:, 8192:9216]
        qs = ["sync", "gpsimd", "scalar"]
        qsem = {"sync": s_aux, "gpsimd": s_gpd, "scalar": s_scd}
        wait("sync", s_cc)
        wait("scalar", s_cc)
        wait("sync", s_vec)
        wait("scalar", s_vec)
        for part in range(2):
            for t in range(16):
                for q in qs:
                    wait(q, s_vec)
                for j in range(8):
                    if part == 0:
                        bl = cck_outF.ap()[j * D + t * 128: j * D + (t + 1) * 128, :]
                        dsl = scr_hk.ap()[t * 128:(t + 1) * 128, :]
                    else:
                        r_, hc = t // 2, t % 2
                        bl = ccv_outF.ap()[j * T + r_ * 128: j * T + (r_ + 1) * 128,
                                           hc * 1024:(hc + 1) * 1024]
                        dsl = scr_hv.ap()[r_ * 128:(r_ + 1) * 128,
                                          hc * 1024:(hc + 1) * 1024]
                    q = qs[j % 3]
                    b.aux_dma(q, qsem[q], stg[j], bl)
                for q in qs:
                    wait("vector", qsem[q])
                vts(accb, stg[0], S3[:, 344:345], None, ALU.mult)
                drain("vector")
                for j in range(1, 8):
                    vstt(accb, stg[j], S3[:, 344 + j:345 + j], accb,
                         ALU.mult, ALU.add, sem=s_vec if j == 7 else None)
                    drain("vector")
                wait("gpsimd", s_vec)
                b.aux_dma("gpsimd", s_gpd, dsl, accb)
        halo_ready = s_gpd.n

        onescol = KONh[:, 384:385]
        op("vector", "memset", ap=onescol, constant=1.0)
        op("vector", "memset", ap=S2[:, 381:382], constant=SWA_BIAS, sem=s_vec)
        drain("vector")
        ET = Eh[:, 0:1152]
        msk = Eh[:, 1152:2304]
        qtl = Eh[:, 2304:3328]
        aw_f = E[:, 1664:1792]
        aw_b = Eh[:, 3584:3712]
        wait("scalar", s_gpd, halo_ready)
        wait("scalar", s_vec)
        wait("scalar", s_ten)
        for hh in range(2):
            wait("scalar", s_ten)
            for w8 in range(8):
                b.aux_dma("scalar", s_scd, Ah[:, w8 * 1024:(w8 + 1) * 1024]
                          .rearrange("p (h s) -> p h s", s=128),
                          scr_hk.ap()[hh * 1024:(hh + 1) * 1024,
                                      w8 * 128:(w8 + 1) * 128]
                          .rearrange("(h p) s -> p h s", p=128))
                b.aux_dma("scalar", s_scd,
                          Ah[:, 8192 + w8 * 1024: 8192 + (w8 + 1) * 1024],
                          scr_hv.ap()[w8 * 128:(w8 + 1) * 128,
                                      hh * 1024:(hh + 1) * 1024])
            for qb in range(NT):
                wait("scalar", s_ten)
                b.aux_dma("scalar", s_scd, msk, swamask_d[qb * 128:(qb + 1) * 128, :])
                b.aux_dma("scalar", s_scd,
                          qtl.rearrange("p (h s) -> p h s", s=128),
                          scr_q.ap()[hh * 1024:(hh + 1) * 1024,
                                     qb * 128:(qb + 1) * 128]
                          .rearrange("(h p) s -> p h s", p=128))
                for hl in range(8):
                    h = hh * 8 + hl
                    wait("tensor", s_scd)
                    wait("tensor", s_vec)
                    for j in range(9):
                        w = qb - 8 + j
                        if w < 0:
                            kb = Ah[:, (w + 8) * 1024 + hl * 128:
                                    (w + 8) * 1024 + (hl + 1) * 128]
                        else:
                            kb = Ch[:, h * 1024 + w * 128:
                                    h * 1024 + (w + 1) * 128]
                        dstp = PA0[:, j * 128:(j + 1) * 128] if j < 8                             else PA1[:, 0:128]
                        mm(dstp, kb, qtl[:, hl * 128:(hl + 1) * 128],
                           start=True, stop=True,
                           sem=s_ten if j == 8 else None)
                    wait("scalar", s_ten)
                    drain("scalar")
                    wait("scalar", s_vec)
                    act(ET[:, 0:1024], PA0[:, 0:1024], AF.Exp,
                        scale=float(DK) ** -0.5, bias=S2[:, 381:382], sem=s_sca)
                    drain("scalar")
                    act(ET[:, 1024:1152], PA1[:, 0:128], AF.Exp,
                        scale=float(DK) ** -0.5, bias=S2[:, 381:382], sem=s_sca)
                    wait("vector", s_sca)
                    wait("vector", s_scd)
                    vtt(ET, ET, msk, ALU.mult, sem=s_vec)
                    drain("vector")
                    wait("tensor", s_vec)
                    for j in range(9):
                        w = qb - 8 + j
                        if w < 0:
                            vb = Ah[:, 8192 + (w + 8) * 1024 + hl * 128:
                                    8192 + (w + 8) * 1024 + (hl + 1) * 128]
                        else:
                            vb = Dh[:, w * 2048 + h * 128:
                                    w * 2048 + (h + 1) * 128]
                        mm(PB[:, 0:128], ET[:, j * 128:(j + 1) * 128], vb,
                           start=(j == 0), stop=(j == 8))
                        mm(PB[:, 512:513], ET[:, j * 128:(j + 1) * 128],
                           onescol, start=(j == 0), stop=(j == 8),
                           sem=s_ten if j == 8 else None)
                    wait("vector", s_ten)
                    op("vector", "reciprocal", out=S2[:, 380:381],
                       in_=PB[:, 512:513])
                    drain("vector")
                    vts(aw_f, PB[:, 0:128], S2[:, 380:381], None, ALU.mult)
                    drain("vector")
                    op("vector", "tensor_copy", out=aw_b, in_=aw_f, sem=s_vec)
                    wait("tensor", s_vec)
                    transpose(PT0[:, 0:128], aw_b, sem=s_ten)
                    wait("vector", s_ten)
                    op("vector", "tensor_copy", sem=s_vec,
                       out=Bh[:, h * 1024 + qb * 128: h * 1024 + (qb + 1) * 128],
                       in_=PT0[:, 0:128])
                    drain("vector")

        wait("sync", s_vec)
        wait("gpsimd", s_vec)
        wait("vector", s_ten)
        for kq in range(64):
            k, qq = kq // 4, kq % 4
            tok = wst.load([(wstg[wst.i % 2],
                             awo_d[k * 128:(k + 1) * 128,
                                   qq * 512:(qq + 1) * 512])])
            p = tok[0]
            wst.wait_tok("vector", tok)
            op("vector", "tensor_copy", sem=s_vec,
               out=wl0(k)[:, qq * 512:(qq + 1) * 512], in_=wstg[p])
            wst.mark_free(tok, s_vec)
        for r in range(NT):
            for rnd in range(2):
                wait("tensor", s_vec)
                for k in range(16):
                    for fo in range(2):
                        o = rnd * 2 + fo
                        mm(PB[:, fo * 512:(fo + 1) * 512],
                           Bh[:, k * 1024 + r * 128: k * 1024 + (r + 1) * 128],
                           wl0(k)[:, o * 512:(o + 1) * 512],
                           start=(k == 0), stop=(k == 15),
                           sem=s_ten if (k == 15 and fo == 1) else None)
                wait("vector", s_ten)
                for fo in range(2):
                    o = rnd * 2 + fo
                    sl = slice(o * 512, (o + 1) * 512)
                    vtt(xt(r)[:, sl], xt(r)[:, sl], PB[:, fo * 512:(fo + 1) * 512],
                        ALU.add, sem=s_vec if fo == 1 else None)
                drain("vector")

    if NSUB >= 3:
        if NSUB >= 4:
            l1_presum()
        phase_l0ffn("p3")
    if NSUB >= 4:
        phase_l1ffn(True)
    if NSUB >= 5:
        phase_swa()
    if NSUB >= 6:
        phase_l1ffn(False)

    wait("sync", s_vec)
    wait("sync", s_sca)
    wait("sync", s_ten)
    for r in range(NT):
        dma("sync", out_d[r * 128:(r + 1) * 128, :], xt(r), s_out)
    wait("sync", s_out)

    return b.finish()



_NC_CACHE = {}


def _host_consts(core):
    base = core * T
    ident = np.eye(128, dtype=np.float32)
    glamask = np.tril(np.ones((CH, CH), np.float32)).T
    m = np.zeros((NT, 128, 1152), np.float32)
    for qb in range(NT):
        for j in range(9):
            kb_global = base + (qb - 8 + j) * 128
            for p in range(128):
                s_glob = kb_global + p
                if s_glob < 0:
                    continue
                qg0 = base + qb * 128
                lo = s_glob - qg0
                hi = s_glob + WIN - qg0
                lo = max(lo, 0)
                hi = min(hi, 128)
                if hi > lo:
                    m[qb, p, j * 128 + lo:j * 128 + hi] = 1.0
    alphas = np.zeros(8, np.float32)
    alphas[:core] = 1.0
    nbr = np.zeros(8, np.float32)
    if core > 0:
        nbr[core - 1] = 1.0
    oddsel = np.array([1.0, 0.0] if core % 2 == 1 else [0.0, 1.0], np.float32)
    pos = (np.arange(2 * T) + base - T).astype(np.float32)
    iota = np.arange(64).astype(np.float32)
    import ml_dtypes
    return {
        "ident": ident.astype(ml_dtypes.bfloat16),
        "glamask": glamask.astype(ml_dtypes.bfloat16),
        "swamask": np.ascontiguousarray(m.reshape(NT * 128, 1152)).astype(ml_dtypes.bfloat16),
        "alphas": alphas, "oddsel": oddsel, "nbr": nbr, "pos": pos,
        "iota64": iota[:, None].copy(),
    }


def kernel(L0_attn_hidden_states, L0_attn_Wq, L0_attn_Wk, L0_attn_Wi,
           L0_attn_Wo, L0_attn_g_weight, L0_ffn_weight, L0_ffn_weight_scale,
           L0_ffn_bias, L1_attn_wq, L1_attn_wk, L1_attn_wv, L1_attn_wo,
           L1_ffn_w1, L1_ffn_w2, L1_ffn_w3):
    x = np.ascontiguousarray(np.asarray(L0_attn_hidden_states, np.float32).reshape(8192, D))
    l0wT = np.ascontiguousarray(np.asarray(L0_ffn_weight, np.float32).T)
    w1T = np.ascontiguousarray(np.asarray(L1_ffn_w1, np.float32).T)
    w2T = np.ascontiguousarray(np.asarray(L1_ffn_w2, np.float32).T)
    w3T = np.ascontiguousarray(np.asarray(L1_ffn_w3, np.float32).T)
    shared = {
        "l0wT": l0wT,
        "l0ws": np.asarray(L0_ffn_weight_scale, np.float32).reshape(1, 1),
        "l0bias": np.asarray(L0_ffn_bias, np.float32),
        "wq": np.ascontiguousarray(np.asarray(L0_attn_Wq, np.float32)),
        "wk": np.ascontiguousarray(np.asarray(L0_attn_Wk, np.float32)),
        "wi": np.ascontiguousarray(np.asarray(L0_attn_Wi, np.float32)),
        "wo": np.ascontiguousarray(np.asarray(L0_attn_Wo, np.float32)),
        "gw": np.asarray(L0_attn_g_weight, np.float32),
        "awq": np.ascontiguousarray(np.asarray(L1_attn_wq, np.float32)),
        "awk": np.ascontiguousarray(np.asarray(L1_attn_wk, np.float32)),
        "awv": np.ascontiguousarray(np.asarray(L1_attn_wv, np.float32)),
        "awo": np.ascontiguousarray(np.asarray(L1_attn_wo, np.float32)),
        "w1T": w1T, "w2T": w2T, "w3T": w3T,
    }
    key = NSUB
    if key not in _NC_CACHE:
        _NC_CACHE[key] = build_program()
    ncp = _NC_CACHE[key]
    in_maps = []
    for c in range(NC):
        m = dict(shared)
        m["x"] = np.ascontiguousarray(x[c * T:(c + 1) * T])
        m.update(_host_consts(c))
        in_maps.append(m)
    res = run_bass_kernel_spmd(ncp, in_maps, core_ids=list(range(NC)))
    out = np.concatenate([np.asarray(r["out"]) for r in res.results], axis=0)
    return out.reshape(1, 8192, D).astype(np.float32)


# version 07 (final)
# speedup vs baseline: 1.1078x; 1.1078x over previous
import os
import numpy as np
import concourse.bass as bass
import concourse.mybir as mybir
from concourse.bass_utils import run_bass_kernel_spmd
from contextlib import ExitStack

F32 = mybir.dt.float32
BF16 = mybir.dt.bfloat16
I32 = mybir.dt.int32
AF = mybir.ActivationFunctionType
ALU = mybir.AluOpType
AX = mybir.AxisListType

NC = 8
T = 1024
D = 2048
NH = 16
DK = 128
CH = 64
NCH = T // CH
DFF = 8192
WIN = 1024
NT = T // 128
NSUB = int(os.environ.get("MK_SUBLAYERS", "6"))

LN_EPS = 1e-5
FP8_MAX = 448.0
SCALE_EPS = 1e-7
QB = 127.0
GEPS = 1e-5
RNE_C = 3.0 * (2.0 ** 22)
SWA_BIAS = -16.0



class Sem:
    def __init__(self, h):
        self.h = h
        self.n = 0


class DStream:

    def __init__(self, b, name, nslots, queue="sync"):
        self.b = b
        self.q = queue
        self.sems = [b.sem(f"{name}{i}") for i in range(nslots)]
        self.n = nslots
        self.i = 0
        self.free_at = [None] * nslots

    def load(self, pairs):
        p = self.i % self.n
        if self.free_at[p] is not None:
            s, c = self.free_at[p]
            self.b.wait(self.q, s, c)
            self.free_at[p] = None
        for out, in_ in pairs:
            self.b.dma(self.q, out, in_, self.sems[p])
        self.i += 1
        return (p, self.sems[p].n)

    def wait_tok(self, eng, tok):
        p, c = tok
        self.b.wait(eng, self.sems[p], c)

    def mark_free(self, tok, sem, count=None):
        p, _ = tok
        self.free_at[p] = (sem, sem.n if count is None else count)


class Builder:
    def __init__(self):
        self.nc = bass.Bass("TRN2", target_bir_lowering=False, debug=False)
        self.es = ExitStack()
        self.sched = []
        self.sems = {}

    def sem(self, name):
        s = Sem(self.es.enter_context(self.nc.semaphore(name)))
        self.sems[name] = s
        return s

    def sb(self, name, shape, dt=F32):
        return self.es.enter_context(self.nc.sbuf_tensor(name, shape, dt))

    def ps(self, name, shape, dt=F32):
        return self.es.enter_context(self.nc.psum_tensor(name, shape, dt))

    def raw(self, eng, fn):
        self.sched.append((eng, fn))

    def wait(self, eng, sem, val=None):
        v = sem.n if val is None else val
        if v <= 0:
            return
        self.raw(eng, lambda e, s=sem.h, v=v: e.wait_ge(s, v))

    def dma(self, eng, out, in_, sem):
        sem.n += 16
        self.raw(eng, lambda e, o=out, i=in_, s=sem.h: e.dma_start(out=o, in_=i).then_inc(s, 16))

    def op(self, eng, meth, sem=None, **kw):
        if sem is not None:
            sem.n += 1
        def f(e, m=meth, kw=kw, s=sem):
            r = getattr(e, m)(**kw)
            if s is not None:
                r.then_inc(s.h, 1)
        self.raw(eng, f)

    def drain(self, eng):
        self.raw(eng, lambda e: e.drain())

    def aux_dma(self, q, sem, out, in_):
        self.dma(q, out, in_, sem)
        self.wait(q, sem)
        return sem.n

    def finish(self):
        nc = self.nc
        sched = self.sched
        with nc.Block() as block:
            @block.sync
            def _(eng):
                for n, f in sched:
                    if n == "sync":
                        f(eng)

            @block.vector
            def _(eng):
                for n, f in sched:
                    if n == "vector":
                        f(eng)

            @block.scalar
            def _(eng):
                for n, f in sched:
                    if n == "scalar":
                        f(eng)

            @block.tensor
            def _(eng):
                for n, f in sched:
                    if n == "tensor":
                        f(eng)

            @block.gpsimd
            def _(eng):
                for n, f in sched:
                    if n == "gpsimd":
                        f(eng)
        self.es.close()
        return nc



def build_program():
    b = Builder()
    nc = b.nc

    def din(name, shape, dt=F32):
        return nc.dram_tensor(name, shape, dt, kind="ExternalInput").ap()

    x_d = din("x", [T, D])
    l0wT_d = din("l0wT", [D, D])
    l0ws_d = din("l0ws", [1, 1])
    l0bias_d = din("l0bias", [D])
    wq_d = din("wq", [D, D])
    wk_d = din("wk", [D, D])
    wi_d = din("wi", [D, D])
    wo_d = din("wo", [D, D])
    gw_d = din("gw", [D])
    awq_d = din("awq", [D, D])
    awk_d = din("awk", [D, D])
    awv_d = din("awv", [D, D])
    awo_d = din("awo", [D, D])
    w1T_d = din("w1T", [D, DFF])
    w2T_d = din("w2T", [D, DFF])
    w3T_d = din("w3T", [DFF, D])
    ident_d = din("ident", [128, 128], BF16)
    glamask_d = din("glamask", [CH, CH], BF16)
    swamask_d = din("swamask", [NT * 128, 1152], BF16)
    alphas_d = din("alphas", [8])
    oddsel_d = din("oddsel", [2])
    nbr_d = din("nbr", [8])
    pos_d = din("pos", [2 * T])
    iota64_d = din("iota64", [64, 1])

    out_d = nc.dram_tensor("out", [T, D], F32, kind="ExternalOutput").ap()

    ccs_in = nc.dram_tensor("ccs_in", [1, 8], F32)
    ccs_out = nc.dram_tensor("ccs_out", [1, 8], F32)
    ccg_in = nc.dram_tensor("ccg_in", [128, 16 + D], F32)
    ccg_out = nc.dram_tensor("ccg_out", [8 * 128, 16 + D], F32)
    cck_in = nc.dram_tensor("cck_in", [D, T], BF16)
    ccv_in = nc.dram_tensor("ccv_in", [T, D], BF16)
    cck_outF = nc.dram_tensor("cck_outF", [8 * D, T], BF16)
    ccv_outF = nc.dram_tensor("ccv_outF", [8 * T, D], BF16)
    scr_q = nc.dram_tensor("scr_q", [D, T], BF16)
    scr_at = nc.dram_tensor("scr_at", [D, T], BF16)

    X = b.sb("X", [128, NT * D])
    A = b.sb("A", [128, 8192])
    B = b.sb("B", [128, 8192])
    Cb = b.sb("Cb", [128, 8192])
    Db = b.sb("Db", [128, 8192])
    E = b.sb("E", [128, 2048])
    S1 = b.sb("S1", [128, 384])
    S2 = b.sb("S2", [128, 384])
    KON = b.sb("KON", [128, 256])
    SI = b.sb("SI", [128, 64], I32)
    S3 = b.sb("S3", [128, 384])

    Bh = B[:].bitcast(BF16)
    Ch = Cb[:].bitcast(BF16)
    Dh = Db[:].bitcast(BF16)
    Ah = A[:].bitcast(BF16)
    Eh = E[:].bitcast(BF16)

    PA0 = b.ps("PA0", [128, 1024])
    PA1 = b.ps("PA1", [128, 1024])
    PB = b.ps("PB", [128, 1024])
    PT0 = b.ps("PT0", [128, 1024], BF16)
    PT1 = b.ps("PT1", [128, 1024], BF16)

    s_syn = b.sem("s_syn")
    s_gpd = b.sem("s_gpd")
    s_scd = b.sem("s_scd")
    s_vec = b.sem("s_vec")
    s_sca = b.sem("s_sca")
    s_ten = b.sem("s_ten")
    s_gpc = b.sem("s_gpc")
    s_cc = b.sem("s_cc")
    s_out = b.sem("s_out")
    s_aux = b.sem("s_aux")

    wait, dma, op, raw, drain = b.wait, b.dma, b.op, b.raw, b.drain
    wst = DStream(b, "wst", 2, queue="sync")

    def vts(out, in0, s1, s2, op0, op1=None, sem=None, eng="vector", accum=None):
        kw = dict(out=out, in0=in0, scalar1=s1, scalar2=s2, op0=op0)
        if op1 is not None:
            kw["op1"] = op1
        if accum is not None:
            kw["accum_out"] = accum
        op(eng, "tensor_scalar", sem=sem, **kw)

    def vtt(out, in0, in1, alu, sem=None, eng="vector"):
        op(eng, "tensor_tensor", sem=sem, out=out, in0=in0, in1=in1, op=alu)

    def vstt(out, in0, scal, in1, op0, op1, sem=None, eng="vector"):
        op(eng, "scalar_tensor_tensor", sem=sem, out=out, in0=in0, scalar=scal,
           in1=in1, op0=op0, op1=op1)

    def act(out, in_, func, bias=0.0, scale=1.0, accum=None, sem=None):
        kw = dict(out=out, in_=in_, func=func, bias=bias, scale=scale)
        if accum is not None:
            kw["accum_out"] = accum
        op("scalar", "activation", sem=sem, **kw)

    def mm(o, lhsT, rhs, start, stop, sem=None):
        op("tensor", "matmul", sem=sem, out=o, lhsT=lhsT, rhs=rhs,
           start=start, stop=stop)

    def transpose(o, in_, sem=None):
        op("tensor", "transpose", sem=sem, out=o, in_=in_, identity=identI)

    KONh = KON[:].bitcast(BF16)
    identI = KONh[:, 0:128]
    glam1 = KONh[0:CH, 128:128 + CH]

    def xt(r):
        return X[:, r * D:(r + 1) * D]

    def rsqrt_small(dst, src, n, eps):
        vts(S2[:, 256:256 + n], src, eps, None, ALU.add)
        drain("vector")
        vi = S2[:, 256:256 + n].bitcast(I32)
        vts(SI[:, 0:n], vi, 1, None, ALU.logical_shift_right)
        drain("vector")
        vts(SI[:, 0:n], SI[:, 0:n], -1, 0x5F3759DF, ALU.mult, ALU.add)
        drain("vector")
        y0 = SI[:, 0:n].bitcast(F32)
        v = S2[:, 256:256 + n]
        cur = y0
        for it in range(3):
            t1 = S2[:, 300:300 + n]
            vtt(t1, cur, cur, ALU.mult)
            drain("vector")
            vstt(t1, t1, -0.5, v, ALU.mult, ALU.mult)
            drain("vector")
            vts(t1, t1, 1.5, None, ALU.add)
            drain("vector")
            nxt = dst if it == 2 else S2[:, 340:340 + n]
            vtt(nxt, cur, t1, ALU.mult)
            drain("vector")
            cur = nxt

    for r in range(NT):
        dma("sync", xt(r), x_d[r * 128:(r + 1) * 128, :], s_syn)
    dma("sync", identI, ident_d[:, :], s_syn)
    dma("sync", glam1, glamask_d[:, :], s_syn)
    x_loaded = s_syn.n

    def wl0(k):
        a = Ch if k < 8 else Dh
        kk = k % 8
        return a[:, kk * 2048:(kk + 1) * 2048]

    def phase_l0ffn(tag):
        wait("scalar", s_vec)
        wait("gpsimd", s_vec)
        wait("gpsimd", s_ten)
        est = [E[:, 0:1024], E[:, 1024:2048]]
        wait("sync", s_vec)
        wait("vector", s_ten)
        for kh in range(32):
            k, h = kh // 2, kh % 2
            tok = wst.load([(est[wst.i % 2],
                             l0wT_d[k * 128:(k + 1) * 128,
                                    h * 1024:(h + 1) * 1024])])
            wst.wait_tok("vector", tok)
            op("vector", "tensor_copy", sem=s_vec,
               out=wl0(k)[:, h * 1024:(h + 1) * 1024], in_=est[tok[0]])
            wst.mark_free(tok, s_vec)
        w_ready = (s_vec.n, s_syn.n)

        b.aux_dma("gpsimd", s_gpd, S1[0:1, 0:1], l0ws_d[0:1, 0:1])
        b.aux_dma("gpsimd", s_gpd, S1[:, 0:1], S1[0:1, 0:1].broadcast_to((1, 128)))
        hb = A[:, 4096:6144]
        b.aux_dma("gpsimd", s_gpd, hb, l0bias_d[None, :].broadcast_to((128, D)))
        wait("vector", s_gpd)
        vts(S1[:, 1:2], S1[:, 0:1], 0.5, None, ALU.mult)
        vts(hb, hb, 0.5, None, ALU.mult)
        drain("vector")

        wait("scalar", s_syn, x_loaded)
        wait("scalar", s_vec)
        for r in range(NT):
            if r > 0:
                drain("scalar")
            act(A[:, 6144:8192], xt(r), AF.Square, accum=S1[:, 48 + r:49 + r],
                sem=s_sca)
        sq_done = s_sca.n
        wait("vector", s_syn, x_loaded)
        for r in range(NT):
            op("vector", "tensor_reduce", out=S1[:, 40:41], in_=xt(r),
               axis=AX.X, op=ALU.add)
            drain("vector")
            vts(S1[:, 16 + r:17 + r], S1[:, 40:41], 1.0 / D, None, ALU.mult)
            drain("vector")
            if r == 0:
                wait("vector", s_sca, sq_done)
            vtt(S1[:, 42:43], S1[:, 16 + r:17 + r], S1[:, 16 + r:17 + r], ALU.mult)
            drain("vector")
            vts(S1[:, 43:44], S1[:, 48 + r:49 + r], 1.0 / D, None, ALU.mult)
            drain("vector")
            vtt(S1[:, 42:43], S1[:, 43:44], S1[:, 42:43], ALU.subtract)
            drain("vector")
            rsqrt_small(S1[:, 24 + r:25 + r], S1[:, 42:43], 1, LN_EPS)
            drain("vector")
            vts(A[:, 0:2048], xt(r), S1[:, 16 + r:17 + r], None, ALU.subtract)
            drain("vector")
            op("vector", "tensor_reduce", out=S1[:, 44:45], in_=A[:, 0:2048],
               axis=AX.X, op=ALU.max, apply_absolute_value=True)
            drain("vector")
            vts(S1[:, 44:45], S1[:, 44:45], S1[:, 24 + r:25 + r], None, ALU.mult)
            drain("vector")
            if r == 0:
                vtt(S1[:, 8:9], S1[:, 44:45], S1[:, 44:45], ALU.max)
            else:
                vtt(S1[:, 8:9], S1[:, 8:9], S1[:, 44:45], ALU.max)
            drain("vector")
        op("vector", "tensor_copy", out=S1[:, 8:9], in_=S1[:, 8:9], sem=s_vec)
        stats_done = s_vec.n

        wait("gpsimd", s_vec, stats_done)
        b.aux_dma("gpsimd", s_gpd, S2[0:1, 0:128], S1[:, 8:9])
        wait("vector", s_gpd)
        op("vector", "tensor_reduce", out=S2[0:1, 128:129], in_=S2[0:1, 0:128],
           axis=AX.X, op=ALU.max, sem=s_vec)
        wait("gpsimd", s_vec)
        b.aux_dma("gpsimd", s_gpd, ccs_in.ap()[0:1, 0:1], S2[0:1, 128:129])
        raw("gpsimd", lambda e: e.collective_compute(
            "AllReduce", ALU.max, replica_groups=[list(range(NC))],
            ins=[ccs_in.ap()[0:1, 0:1].opt()],
            outs=[ccs_out.ap()[0:1, 0:1].opt()]).then_inc(s_cc.h, 1))
        s_cc.n += 1
        wait("gpsimd", s_cc)
        b.aux_dma("gpsimd", s_gpd, S2[:, 129:130],
                  ccs_out.ap()[0:1, 0:1].broadcast_to((1, 128)))
        wait("vector", s_gpd)
        vts(S2[:, 130:131], S2[:, 129:130], 1.0 / FP8_MAX, SCALE_EPS,
            ALU.mult, ALU.max)
        drain("vector")
        op("vector", "reciprocal", out=S2[:, 131:132], in_=S2[:, 130:131])
        drain("vector")

        for r in range(NT):
            vts(S1[:, 36:37], S1[:, 24 + r:25 + r], S2[:, 131:132], None,
                ALU.mult)
            drain("vector")
            vts(A[:, 0:2048], xt(r), S1[:, 16 + r:17 + r], None, ALU.subtract)
            drain("vector")
            vts(A[:, 0:2048], A[:, 0:2048], S1[:, 36:37], None, ALU.mult)
            drain("vector")
            ui = A[:, 0:2048].bitcast(I32)
            ti = A[:, 2048:4096].bitcast(I32)
            vts(ti, ui, 0x7FFFFFFF, 23, ALU.bitwise_and, ALU.logical_shift_right)
            drain("vector")
            vts(ti, ti, 121, 20, ALU.max, ALU.add)
            drain("vector")
            vts(ti, ti, 23, 0x00400000, ALU.logical_shift_left, ALU.bitwise_or)
            drain("vector")
            tif = A[:, 2048:4096]
            vtt(A[:, 0:2048], A[:, 0:2048], tif, ALU.add)
            drain("vector")
            xq = Ah[:, 12288:14336]
            vtt(xq, A[:, 0:2048], tif, ALU.subtract, sem=s_vec)
            xq_done = s_vec.n
            wait("tensor", s_vec, xq_done)
            for k in range(16):
                pc = (PT0, PT1)[k % 2]
                if k >= 2:
                    wait("tensor", s_vec)
                transpose(pc[:, 0:128], xq[:, k * 128:(k + 1) * 128], sem=s_ten)
                wait("vector", s_ten)
                op("vector", "tensor_copy", sem=s_vec,
                   out=Bh[:, k * 128:(k + 1) * 128], in_=pc[:, 0:128])
            wait("tensor", s_vec)
            if r == 0:
                wait("tensor", s_vec, w_ready[0])
            for rnd in range(2):
                if rnd == 1 or r > 0:
                    wait("tensor", s_vec)
                for k in range(16):
                    for fo in range(2):
                        o = rnd * 2 + fo
                        mm(PB[:, fo * 512:(fo + 1) * 512],
                           Bh[:, k * 128:(k + 1) * 128],
                           wl0(k)[:, o * 512:(o + 1) * 512],
                           start=(k == 0), stop=(k == 15),
                           sem=s_ten if (k == 15 and fo == 1) else None)
                wait("vector", s_ten)
                for fo in range(2):
                    o = rnd * 2 + fo
                    sl = slice(o * 512, (o + 1) * 512)
                    psl = slice(fo * 512, (fo + 1) * 512)
                    vstt(A[:, sl], PB[:, psl], S1[:, 1:2], hb[:, sl],
                         ALU.mult, ALU.add)
                drain("vector")
                for fo in range(2):
                    o = rnd * 2 + fo
                    sl = slice(o * 512, (o + 1) * 512)
                    vtt(xt(r)[:, sl], xt(r)[:, sl], A[:, sl], ALU.add,
                        sem=s_vec if fo == 1 else None)
                drain("vector")

    phase_l0ffn("p1")

    op("vector", "memset", ap=KON[:, 96:97], constant=1.0)
    drain("vector")

    def xlnT(k):
        return Ah[:, k * 1024:(k + 1) * 1024]

    def phase_gla():
        wait("scalar", s_vec)
        for r in range(NT):
            drain("scalar")
            act(A[:, 6144:8192], xt(r), AF.Square, accum=S1[:, 48 + r:49 + r],
                sem=s_sca)
        sq_done = s_sca.n
        for r in range(NT):
            op("vector", "tensor_reduce", out=S1[:, 40:41], in_=xt(r),
               axis=AX.X, op=ALU.add)
            drain("vector")
            vts(S1[:, 16 + r:17 + r], S1[:, 40:41], 1.0 / D, None, ALU.mult)
            drain("vector")
            if r == 0:
                wait("vector", s_sca, sq_done)
            vtt(S1[:, 42:43], S1[:, 16 + r:17 + r], S1[:, 16 + r:17 + r], ALU.mult)
            drain("vector")
            vts(S1[:, 43:44], S1[:, 48 + r:49 + r], 1.0 / D, None, ALU.mult)
            drain("vector")
            vtt(S1[:, 42:43], S1[:, 43:44], S1[:, 42:43], ALU.subtract)
            drain("vector")
            rsqrt_small(S1[:, 24 + r:25 + r], S1[:, 42:43], 1, LN_EPS)
            drain("vector")

        for j in range(8):
            b.aux_dma("gpsimd", s_gpd, S3[:, 288 + j:289 + j],
                      alphas_d[None, j:j + 1].broadcast_to((1, 128)))

        xlnf = E[:, 0:2048]
        xlh = Dh[:, 0:2048]
        for r in range(NT):
            vts(xlnf, xt(r), S1[:, 16 + r:17 + r], None, ALU.subtract)
            drain("vector")
            vts(xlnf, xlnf, S1[:, 24 + r:25 + r], None, ALU.mult)
            drain("vector")
            op("vector", "tensor_copy", out=xlh, in_=xlnf, sem=s_vec)
            wait("tensor", s_vec)
            for k in range(16):
                pc = (PT0, PT1)[k % 2]
                if k >= 2:
                    wait("tensor", s_vec)
                transpose(pc[:, 0:128], xlh[:, k * 128:(k + 1) * 128], sem=s_ten)
                wait("vector", s_ten)
                op("vector", "tensor_copy", sem=s_vec,
                   out=xlnT(k)[:, r * 128:(r + 1) * 128], in_=pc[:, 0:128])
            drain("vector")

        wstg = [E[:, 0:512], E[:, 512:1024]]
        wbf = [Eh[:, 3072:3584], Eh[:, 3584:4096]]
        wbf_free = {0: 0, 1: 0}

        def stream_wtile(w_d, k, c0, w):
            tok = wst.load([(wstg[wst.i % 2][:, 0:w],
                             w_d[k * 128:(k + 1) * 128, c0:c0 + w])])
            p = tok[0]
            wst.wait_tok("vector", tok)
            wait("vector", s_ten, wbf_free[p])
            op("vector", "tensor_copy", sem=s_vec, out=wbf[p][:, 0:w],
               in_=wstg[p][:, 0:w])
            wst.mark_free(tok, s_vec)
            return p, s_vec.n

        def wtile_done(p):
            wbf_free[p] = s_ten.n

        onesc = KON[:, 96:97]
        g_eza = Db[:, 2560:3072]
        g_dec = Db[:, 3072:3584]
        g_k = Db[:, 3584:4096]
        g_cp = Db[:, 4096:4608]
        g_rcp = Db[:, 4608:5120]
        paslot = [PA0[:, 0:512], PA0[:, 512:1024], PA1[:, 0:512], PA1[:, 512:1024]]

        wait("sync", s_vec)
        for og in range(8):
            wait("tensor", s_vec)
            for dm in range(16):
                p_, wcnt = stream_wtile(wk_d, dm, og * 256, 256)
                wb = wbf[p_]
                wait("tensor", s_vec, wcnt)
                for oo in range(2):
                    for th in range(2):
                        mm(paslot[oo * 2 + th],
                           wb[:, oo * 128:(oo + 1) * 128],
                           xlnT(dm)[:, th * 512:(th + 1) * 512],
                           start=(dm == 0), stop=(dm == 15),
                           sem=s_ten if (oo == 1 and th == 1) else None)
                wtile_done(p_)
            wait("scalar", s_ten)
            for oo in range(2):
                ot = og * 2 + oo
                for th in range(2):
                    ps_ = paslot[oo * 2 + th]
                    drain("scalar")
                    wait("scalar", s_vec)
                    act(g_eza, ps_, AF.Exp, sem=s_sca)
                    wait("vector", s_sca)
                    vts(g_eza, g_eza, 1.0, None, ALU.add)
                    drain("vector")
                    op("vector", "reciprocal", out=g_dec, in_=g_eza)
                    drain("vector")
                    vts(g_k, g_dec, 1.0, -1.0, ALU.subtract, ALU.mult)
                    drain("vector")
                    for c in range(8):
                        op("vector", "tensor_tensor_scan",
                           out=g_cp[:, c * 64:(c + 1) * 64],
                           data0=g_dec[:, c * 64:(c + 1) * 64],
                           data1=onesc.broadcast_to((128, 64)),
                           initial=1.0, op0=ALU.mult, op1=ALU.mult)
                    drain("vector")
                    op("vector", "tensor_copy",
                       out=Bh[:, ot * 1024 + th * 512: ot * 1024 + (th + 1) * 512],
                       in_=g_cp)
                    op("vector", "tensor_copy",
                       out=S3[:, ot * 16 + th * 8: ot * 16 + (th + 1) * 8]
                           .rearrange("p (c o) -> p c o", o=1),
                       in_=g_cp.rearrange("p (c s) -> p c s", s=64)[:, :, 63:64])
                    drain("vector")
                    op("vector", "reciprocal", out=g_rcp, in_=g_cp)
                    drain("vector")
                    vtt(Ch[:, ot * 1024 + th * 512: ot * 1024 + (th + 1) * 512],
                        g_k, g_rcp, ALU.mult, sem=s_vec)
                    drain("vector")

        for rg in range(2):
            for dv in range(4):
                wait("tensor", s_vec)
                for dm in range(16):
                    p_, wcnt = stream_wtile(wi_d, dm, dv * 512, 512)
                    wb = wbf[p_]
                    wait("tensor", s_vec, wcnt)
                    for rr in range(4):
                        r = rg * 4 + rr
                        mm(paslot[rr],
                           xlnT(dm)[:, r * 128:(r + 1) * 128],
                           wb[:, 0:512],
                           start=(dm == 0), stop=(dm == 15),
                           sem=s_ten if rr == 3 else None)
                    wtile_done(p_)
                wait("vector", s_ten)
                for rr in range(4):
                    r = rg * 4 + rr
                    op("vector", "tensor_copy", sem=s_vec,
                       out=Dh[:, r * 2048 + dv * 512: r * 2048 + (dv + 1) * 512],
                       in_=paslot[rr])
                drain("vector")

        cpc3 = S3[:, 0:256].rearrange("p (h i) -> p h i", i=16)
        sview = A[:, 0:2048].rearrange("p (h v) -> p h v", v=128)
        eview = E[:, 0:2048].rearrange("p (h v) -> p h v", v=128)
        k2tmp = KONh[:, 192:256]
        op("vector", "memset", ap=E[:, 0:2048], constant=0.0, sem=s_vec)
        drain("vector")

        ch3 = Ch.rearrange("p (h t) -> p h t", t=1024)
        k2a = KONh[:, 192:448]
        k2a3 = k2a.rearrange("p (h t) -> p h t", t=64)
        k2T4 = S2[:, 0:256].bitcast(BF16)

        def state_chunk(i, dst_f32, dview, into_sem):
            ob = 64 * (i % 2)
            rr = i // 2
            for hg in range(4):
                vtt(k2a3, ch3[:, hg * 4:hg * 4 + 4, i * 64:(i + 1) * 64],
                    cpc3[:, hg * 4:hg * 4 + 4, i:i + 1]
                    .broadcast_to((128, 4, 64)), ALU.mult, sem=s_vec)
                drain("vector")
                wait("tensor", s_vec)
                for hh in range(4):
                    transpose(PT0[ob:ob + 64, hh * 128:(hh + 1) * 128],
                              k2a[:, hh * 64:(hh + 1) * 64],
                              sem=s_ten if hh == 3 else None)
                wait("vector", s_ten)
                op("vector", "tensor_copy", sem=s_vec,
                   out=k2T4[ob:ob + 64, :], in_=PT0[ob:ob + 64, 0:512])
                drain("vector")
                wait("tensor", s_vec)
                for hh in range(4):
                    h = hg * 4 + hh
                    pa_ = PA0 if h < 8 else PA1
                    mm(pa_[:, (h % 8) * 128:(h % 8 + 1) * 128],
                       k2T4[ob:ob + 64, hh * 128:(hh + 1) * 128],
                       Dh[ob:ob + 64,
                          rr * 2048 + h * 128: rr * 2048 + (h + 1) * 128],
                       start=True, stop=True,
                       sem=s_ten if hh == 3 else None)
            wait("vector", s_ten)
            vtt(dview, dview,
                cpc3[:, :, i:i + 1].broadcast_to((128, 16, 128)), ALU.mult)
            drain("vector")
            vtt(dst_f32[:, 0:1024], dst_f32[:, 0:1024], PA0[:, 0:1024], ALU.add)
            drain("vector")
            vtt(dst_f32[:, 1024:2048], dst_f32[:, 1024:2048], PA1[:, 0:1024],
                ALU.add, sem=into_sem)
            drain("vector")

        for i in range(NCH):
            state_chunk(i, E[:, 0:2048], eview, s_vec)

        lam3 = S3[:, 256:272].rearrange("p (h o) -> p h o", o=1)
        op("vector", "tensor_copy", out=lam3, in_=cpc3[:, :, 0:1])
        drain("vector")
        for i in range(1, 16):
            vtt(lam3, lam3, cpc3[:, :, i:i + 1], ALU.mult)
            drain("vector")
        op("vector", "tensor_copy", out=S3[:, 272:273], in_=S3[:, 256:257],
           sem=s_vec)
        wait("gpsimd", s_vec)
        b.aux_dma("gpsimd", s_gpd, ccg_in.ap()[:, 0:16], S3[:, 256:272])
        b.aux_dma("gpsimd", s_gpd, ccg_in.ap()[:, 16:16 + D], E[:, 0:2048])
        pack_cnt = s_gpd.n
        raw("gpsimd", lambda e: e.collective_compute(
            "AllGather", ALU.bypass, replica_groups=[list(range(NC))],
            ins=[ccg_in.ap()[:, :].opt()],
            outs=[ccg_out.ap()[:, :].opt()]).then_inc(s_cc.h, 1))
        s_cc.n += 1

        wait("sync", s_gpd, pack_cnt)
        qtmp = E[:, 1024:1536]
        for og in range(8):
            wait("tensor", s_vec)
            for dm in range(16):
                p_, wcnt = stream_wtile(wq_d, dm, og * 256, 256)
                wb = wbf[p_]
                wait("tensor", s_vec, wcnt)
                for oo in range(2):
                    for th in range(2):
                        mm(paslot[oo * 2 + th],
                           wb[:, oo * 128:(oo + 1) * 128],
                           xlnT(dm)[:, th * 512:(th + 1) * 512],
                           start=(dm == 0), stop=(dm == 15),
                           sem=s_ten if (oo == 1 and th == 1) else None)
                wtile_done(p_)
            wait("scalar", s_ten)
            for oo in range(2):
                ot = og * 2 + oo
                for th in range(2):
                    drain("scalar")
                    wait("scalar", s_vec)
                    act(qtmp, paslot[oo * 2 + th], AF.Relu,
                        scale=float(DK) ** -0.5, sem=s_sca)
                    wait("vector", s_sca)
                    sl = slice(ot * 1024 + th * 512, ot * 1024 + (th + 1) * 512)
                    vtt(Bh[:, sl], qtmp, Bh[:, sl], ALU.mult, sem=s_vec)
                    drain("vector")

        wait("gpsimd", s_cc)
        wait("gpsimd", s_ten)
        wait("gpsimd", s_vec)
        op("vector", "memset", ap=A[:, 0:2048], constant=0.0)
        op("vector", "memset", ap=S3[:, 304:320], constant=1.0)
        vts(S3[:, 296:304], S3[:, 288:296], -1.0, 1.0, ALU.mult, ALU.add)
        drain("vector")
        rview = S3[:, 304:320].rearrange("p (h o) -> p h o", o=1)
        for j in range(6, -1, -1):
            wait("gpsimd", s_vec)
            b.aux_dma("gpsimd", s_gpd, A[:, 2048:2048 + 2064],
                      ccg_out.ap()[j * 128:(j + 1) * 128, :])
            wait("vector", s_gpd)
            sj = A[:, 2064:2064 + 2048].rearrange("p (h v) -> p h v", v=128)
            t3 = A[:, 4112:4112 + 2048].rearrange("p (h v) -> p h v", v=128)
            vtt(t3, sj, rview.broadcast_to((128, 16, 128)), ALU.mult)
            drain("vector")
            vstt(A[:, 0:2048], A[:, 4112:4112 + 2048], S3[:, 288 + j:289 + j],
                 A[:, 0:2048], ALU.mult, ALU.add)
            drain("vector")
            lamj = A[:, 2048:2064]
            vts(S3[:, 320:336], lamj, S3[:, 288 + j:289 + j], None, ALU.mult)
            drain("vector")
            vts(S3[:, 320:336], S3[:, 320:336], S3[:, 296 + j:297 + j], None,
                ALU.add)
            drain("vector")
            vtt(S3[:, 304:320], S3[:, 304:320], S3[:, 320:336], ALU.mult,
                sem=s_vec)
            drain("vector")
        wait("gpsimd", s_vec)
        b.aux_dma("gpsimd", s_gpd, A[:, 5120:7168],
                  gw_d[None, :].broadcast_to((128, D)))
        wait("vector", s_gpd)

        g_rep = A[:, 5120:7168]
        ost = A[:, 3072:5120]
        sbf = Ah[:, 4096:6144]
        for i in range(NCH):
            ob = 64 * (i % 2)
            rr = i // 2
            op("vector", "tensor_copy", out=sbf, in_=A[:, 0:2048], sem=s_vec)
            drain("vector")
            for hh in range(2):
                wait("tensor", s_vec)
                for ho in range(8):
                    h = hh * 8 + ho
                    mm(PA0[ob:ob + 64, ho * 64:(ho + 1) * 64],
                       Ch[:, h * 1024 + i * 64: h * 1024 + (i + 1) * 64],
                       Bh[:, h * 1024 + i * 64: h * 1024 + (i + 1) * 64],
                       start=True, stop=True, sem=s_ten if ho == 7 else None)
                wait("vector", s_ten)
                msb = Eh[ob:ob + 64, 0:512]
                vtt(msb.rearrange("p (c s) -> p c s", s=64),
                    PA0[ob:ob + 64, 0:512].rearrange("p (c s) -> p c s", s=64),
                    glam1.rearrange("s (o t) -> s o t", o=1)
                        .broadcast_to((64, 8, 64)),
                    ALU.mult, sem=s_vec)
                drain("vector")
                wait("tensor", s_vec)
                for ho in range(8):
                    h = hh * 8 + ho
                    mm(PB[ob:ob + 64, ho * 128:(ho + 1) * 128],
                       msb[:, ho * 64:(ho + 1) * 64],
                       Dh[ob:ob + 64,
                          rr * 2048 + h * 128: rr * 2048 + (h + 1) * 128],
                       start=True, stop=False)
                    mm(PB[ob:ob + 64, ho * 128:(ho + 1) * 128],
                       Bh[:, h * 1024 + i * 64: h * 1024 + (i + 1) * 64],
                       sbf[:, h * 128:(h + 1) * 128],
                       start=False, stop=True, sem=s_ten if ho == 7 else None)
                wait("vector", s_ten)
                op("vector", "tensor_copy", sem=s_vec,
                   out=ost[ob:ob + 64, hh * 1024:(hh + 1) * 1024],
                   in_=PB[ob:ob + 64, 0:1024])
                drain("vector")
            state_chunk(i, A[:, 0:2048], sview, s_vec)
            if i % 2 == 1:
                p = i // 2
                drain("scalar")
                wait("scalar", s_vec)
                act(E[:, 1024:2048], ost[:, 0:1024], AF.Square,
                    accum=S2[:, 350:351], sem=s_sca)
                drain("scalar")
                act(E[:, 1024:2048], ost[:, 1024:2048], AF.Square,
                    accum=S2[:, 351:352], sem=s_sca)
                wait("vector", s_sca)
                vtt(S2[:, 352:353], S2[:, 350:351], S2[:, 351:352], ALU.add)
                drain("vector")
                vts(S2[:, 352:353], S2[:, 352:353], 1.0 / D, None, ALU.mult)
                drain("vector")
                rsqrt_small(S2[:, 353:354], S2[:, 352:353], 1, LN_EPS)
                drain("vector")
                vts(ost, ost, S2[:, 353:354], None, ALU.mult)
                drain("vector")
                onh = Eh[:, 2048:4096]
                vtt(onh, ost, g_rep, ALU.mult, sem=s_vec)
                drain("vector")
                wait("tensor", s_vec)
                for k in range(16):
                    pc = (PT0, PT1)[k % 2]
                    if k >= 2:
                        wait("tensor", s_vec)
                    transpose(pc[:, 0:128], onh[:, k * 128:(k + 1) * 128],
                              sem=s_ten)
                    wait("vector", s_ten)
                    op("vector", "tensor_copy", sem=s_vec,
                       out=Bh[:, k * 1024 + p * 128: k * 1024 + (p + 1) * 128],
                       in_=pc[:, 0:128])
                drain("vector")

        wait("sync", s_vec)
        wait("vector", s_ten)
        for kq in range(64):
            k, qq = kq // 4, kq % 4
            tok = wst.load([(wstg[wst.i % 2],
                             wo_d[k * 128:(k + 1) * 128,
                                  qq * 512:(qq + 1) * 512])])
            p = tok[0]
            wst.wait_tok("vector", tok)
            op("vector", "tensor_copy", sem=s_vec,
               out=wl0(k)[:, qq * 512:(qq + 1) * 512], in_=wstg[p])
            wst.mark_free(tok, s_vec)
        for r in range(NT):
            for rnd in range(2):
                wait("tensor", s_vec)
                for k in range(16):
                    for fo in range(2):
                        o = rnd * 2 + fo
                        mm(PB[:, fo * 512:(fo + 1) * 512],
                           Bh[:, k * 1024 + r * 128: k * 1024 + (r + 1) * 128],
                           wl0(k)[:, o * 512:(o + 1) * 512],
                           start=(k == 0), stop=(k == 15),
                           sem=s_ten if (k == 15 and fo == 1) else None)
                wait("vector", s_ten)
                for fo in range(2):
                    o = rnd * 2 + fo
                    sl = slice(o * 512, (o + 1) * 512)
                    vtt(xt(r)[:, sl], xt(r)[:, sl], PB[:, fo * 512:(fo + 1) * 512],
                        ALU.add, sem=s_vec if fo == 1 else None)
                drain("vector")

    if NSUB >= 2:
        phase_gla()


    def l1_presum():
        wpre = DStream(b, "wpre", 2, queue="scalar")
        pstg = [B[:, 4096:4608], B[:, 4608:5120]]
        junk = B[:, 5120:5632]
        accp = [B[:, 5640:5641], B[:, 5641:5642]]
        runc = [B[:, 5644:5645], B[:, 5645:5646], B[:, 5646:5647]]
        op("vector", "memset", ap=B[:, 5644:5647], constant=0.0, sem=s_vec)
        drain("vector")
        wait("scalar", s_ten)
        acc_gate = [0, 0]
        for widx, w_d in ((0, w1T_d), (1, w2T_d), (2, w3T_d)):
            ncols = 16 if widx < 2 else 4
            for t in range(256):
                r, c = divmod(t, ncols)
                tok = wpre.load([(pstg[wpre.i % 2],
                                  w_d[r * 128:(r + 1) * 128,
                                      c * 512:(c + 1) * 512])])
                p = tok[0]
                wpre.wait_tok("scalar", tok)
                drain("scalar")
                wait("scalar", s_vec, acc_gate[p])
                act(junk, pstg[p], AF.Abs, accum=accp[p], sem=s_sca)
                wpre.mark_free(tok, s_sca)
                wait("vector", s_sca)
                vtt(runc[widx], runc[widx], accp[p], ALU.add, sem=s_vec)
                drain("vector")
                acc_gate[p] = s_vec.n
        wait("gpsimd", s_vec)
        for widx in range(3):
            b.aux_dma("gpsimd", s_gpd, B[0:1, 5660:5788], runc[widx])
            wait("vector", s_gpd)
            op("vector", "tensor_reduce", out=B[0:1, 5800 + widx:5801 + widx],
               in_=B[0:1, 5660:5788], axis=AX.X, op=ALU.add, sem=s_vec)
            drain("vector")
            wait("gpsimd", s_vec)
            b.aux_dma("gpsimd", s_gpd, S3[:, 336 + widx:337 + widx],
                      B[0:1, 5800 + widx:5801 + widx].broadcast_to((1, 128)))
        wait("vector", s_gpd)
        for widx in range(3):
            vts(S3[:, 336 + widx:337 + widx], S3[:, 336 + widx:337 + widx],
                float(2.0 ** -24), None, ALU.mult)
        drain("vector")
        vts(S3[:, 339:340], S3[:, 338:339], 0.5, None, ALU.mult, sem=s_vec)
        drain("vector")

    def phase_l1ffn(first):
        wait("scalar", s_vec)
        wait("gpsimd", s_vec)
        wait("gpsimd", s_ten)
        for r in range(NT):
            drain("scalar")
            wait("scalar", s_vec)
            act(A[:, 6144:8192], xt(r), AF.Square, accum=S1[:, 48 + r:49 + r],
                sem=s_sca)
        sq_done = s_sca.n
        for r in range(NT):
            op("vector", "tensor_reduce", out=S1[:, 40:41], in_=xt(r),
               axis=AX.X, op=ALU.add)
            drain("vector")
            vts(S1[:, 16 + r:17 + r], S1[:, 40:41], 1.0 / D, None, ALU.mult)
            drain("vector")
            if r == 0:
                wait("vector", s_sca, sq_done)
            vtt(S1[:, 42:43], S1[:, 16 + r:17 + r], S1[:, 16 + r:17 + r], ALU.mult)
            drain("vector")
            vts(S1[:, 43:44], S1[:, 48 + r:49 + r], 1.0 / D, None, ALU.mult)
            drain("vector")
            vtt(S1[:, 42:43], S1[:, 43:44], S1[:, 42:43], ALU.subtract)
            drain("vector")
            rsqrt_small(S1[:, 24 + r:25 + r], S1[:, 42:43], 1, LN_EPS)
            drain("vector")

        def quant_tile(dst_b16, src_f32, W):
            ng = W // 64
            s3v = src_f32.rearrange("p (g k) -> p g k", k=64)
            d3v = dst_b16.rearrange("p (g k) -> p g k", k=64)
            op("vector", "tensor_reduce", out=S2[:, 0:ng], in_=s3v,
               axis=AX.X, op=ALU.max, apply_absolute_value=True)
            drain("vector")
            vts(S2[:, 0:ng], S2[:, 0:ng], GEPS, None, ALU.max)
            drain("vector")
            op("vector", "reciprocal", out=S2[:, 128:128 + ng], in_=S2[:, 0:ng])
            drain("vector")
            vts(S2[:, 128:128 + ng], S2[:, 128:128 + ng], QB, None, ALU.mult)
            drain("vector")
            vts(S2[:, 0:ng], S2[:, 0:ng], 1.0 / QB, None, ALU.mult)
            drain("vector")
            sc3a = S2[:, 128:128 + ng].rearrange("p (g o) -> p g o", o=1)
            sc3b = S2[:, 0:ng].rearrange("p (g o) -> p g o", o=1)
            vtt(s3v, s3v, sc3a.broadcast_to((128, ng, 64)), ALU.mult)
            drain("vector")
            vts(src_f32, src_f32, RNE_C, RNE_C, ALU.add, ALU.subtract)
            drain("vector")
            vtt(d3v, s3v, sc3b.broadcast_to((128, ng, 64)), ALU.mult, sem=s_vec)
            drain("vector")

        op("vector", "memset", ap=S2[:, 200:201], constant=0.0)
        drain("vector")
        xqh = Dh[:, 0:2048]
        for r in range(NT):
            vts(A[:, 0:2048], xt(r), S1[:, 16 + r:17 + r], None, ALU.subtract)
            drain("vector")
            vts(A[:, 0:2048], A[:, 0:2048], S1[:, 24 + r:25 + r], None, ALU.mult)
            drain("vector")
            quant_tile(xqh, A[:, 0:2048], D)
            wait("tensor", s_vec)
            for k in range(16):
                pc = (PT0, PT1)[k % 2]
                if k >= 2:
                    wait("tensor", s_vec)
                transpose(pc[:, 0:128], xqh[:, k * 128:(k + 1) * 128], sem=s_ten)
                wait("vector", s_ten)
                op("vector", "tensor_copy", sem=s_vec,
                   out=Bh[:, k * 1024 + r * 128: k * 1024 + (r + 1) * 128],
                   in_=pc[:, 0:128])
            drain("vector")

        wstg = [E[:, 0:512], E[:, 512:1024]]
        wbf = [Eh[:, 3072:3584], Eh[:, 3584:4096]]
        l1_free = {0: 0, 1: 0}

        def l1_wtile(w_d, r0, c0):
            tok = wst.load([(wstg[wst.i % 2],
                             w_d[r0:r0 + 128, c0:c0 + 512])])
            p = tok[0]
            wst.wait_tok("scalar", tok)
            drain("scalar")
            wait("scalar", s_ten, l1_free[p])
            act(wbf[p], wstg[p], AF.Sign, sem=s_sca)
            wst.mark_free(tok, s_sca)
            return p, s_sca.n

        paslot2 = [PA0[:, 0:512], PA0[:, 512:1024], PA1[:, 0:512],
                   PA1[:, 512:1024]]
        y2a = [A[:, 0:2048], A[:, 2048:4096], A[:, 4096:6144], A[:, 6144:8192]]
        wait("sync", s_vec)
        wait("sync", s_ten)
        for rg in range(2):
            for q in range(4):
                for iw, w_d in ((2, w2T_d), (1, w1T_d)):
                    for dffc in range(4):
                        wait("tensor", s_vec)
                        wait("tensor", s_sca)
                        c0 = q * 2048 + dffc * 512
                        for dm in range(16):
                            p_, wcnt = l1_wtile(w_d, dm * 128, c0)
                            wait("tensor", s_sca, wcnt)
                            for tt in range(4):
                                r = rg * 4 + tt
                                mm(paslot2[tt],
                                   Bh[:, dm * 1024 + r * 128:
                                      dm * 1024 + (r + 1) * 128],
                                   wbf[p_],
                                   start=(dm == 0), stop=(dm == 15),
                                   sem=s_ten if tt == 3 else None)
                            l1_free[p_] = s_ten.n
                        sl = slice(dffc * 512, (dffc + 1) * 512)
                        if iw == 2:
                            wait("scalar", s_ten)
                            for tt in range(4):
                                drain("scalar")
                                wait("scalar", s_vec)
                                act(y2a[tt][:, sl], paslot2[tt], AF.Copy,
                                    scale=S3[:, 337:338], sem=s_sca)
                            wait("vector", s_sca)
                            op("vector", "tensor_copy", out=S2[:, 200:201],
                               in_=S2[:, 200:201], sem=s_vec)
                        else:
                            wait("scalar", s_ten)
                            for tt in range(4):
                                drain("scalar")
                                wait("scalar", s_vec)
                                act(E[:, 1024:1536], paslot2[tt], AF.Silu,
                                    scale=S3[:, 336:337], sem=s_sca)
                                wait("vector", s_sca)
                                vtt(y2a[tt][:, sl], E[:, 1024:1536],
                                    y2a[tt][:, sl], ALU.mult, sem=s_vec)
                                drain("vector")
                for tt in range(4):
                    hq = Dh[:, 8192 + (tt % 2) * 2048: 8192 + (tt % 2 + 1) * 2048]
                    quant_tile(hq, y2a[tt], 2048)
                    wait("tensor", s_vec)
                    for k in range(16):
                        pc = (PT0, PT1)[k % 2]
                        if k >= 2:
                            wait("tensor", s_vec)
                        transpose(pc[:, 0:128], hq[:, k * 128:(k + 1) * 128],
                                  sem=s_ten)
                        wait("vector", s_ten)
                        op("vector", "tensor_copy", sem=s_vec,
                           out=Dh[:, k * 512 + tt * 128: k * 512 + (tt + 1) * 128],
                           in_=pc[:, 0:128])
                    drain("vector")
                for fo in range(4):
                    wait("tensor", s_vec)
                    for dffc in range(16):
                        p_, wcnt = l1_wtile(w3T_d, q * 2048 + dffc * 128,
                                            fo * 512)
                        wait("tensor", s_sca, wcnt)
                        for tt in range(4):
                            mm(paslot2[tt],
                               Dh[:, dffc * 512 + tt * 128:
                                  dffc * 512 + (tt + 1) * 128],
                               wbf[p_],
                               start=(dffc == 0), stop=(dffc == 15),
                               sem=s_ten if tt == 3 else None)
                        l1_free[p_] = s_ten.n
                    wait("vector", s_ten)
                    for tt in range(4):
                        zsl = Cb[:, tt * 2048 + fo * 512:
                                 tt * 2048 + (fo + 1) * 512]
                        if q == 0:
                            op("vector", "tensor_copy", out=zsl,
                               in_=paslot2[tt],
                               sem=s_vec if tt == 3 else None)
                        else:
                            vtt(zsl, zsl, paslot2[tt], ALU.add,
                                sem=s_vec if tt == 3 else None)
                    drain("vector")
            for tt in range(4):
                r = rg * 4 + tt
                vstt(xt(r), Cb[:, tt * 2048:(tt + 1) * 2048], S3[:, 339:340],
                     xt(r), ALU.mult, ALU.add, sem=s_vec)
            drain("vector")

    scr_hk = nc.dram_tensor("scr_hk", [D, T], BF16)
    scr_hv = nc.dram_tensor("scr_hv", [T, D], BF16)

    def phase_swa():
        wait("scalar", s_vec)
        wait("gpsimd", s_vec)
        wait("gpsimd", s_ten)
        for r in range(NT):
            drain("scalar")
            wait("scalar", s_vec)
            act(A[:, 6144:8192], xt(r), AF.Square, accum=S1[:, 48 + r:49 + r],
                sem=s_sca)
        sq_done = s_sca.n
        for r in range(NT):
            op("vector", "tensor_reduce", out=S1[:, 40:41], in_=xt(r),
               axis=AX.X, op=ALU.add)
            drain("vector")
            vts(S1[:, 16 + r:17 + r], S1[:, 40:41], 1.0 / D, None, ALU.mult)
            drain("vector")
            if r == 0:
                wait("vector", s_sca, sq_done)
            vtt(S1[:, 42:43], S1[:, 16 + r:17 + r], S1[:, 16 + r:17 + r], ALU.mult)
            drain("vector")
            vts(S1[:, 43:44], S1[:, 48 + r:49 + r], 1.0 / D, None, ALU.mult)
            drain("vector")
            vtt(S1[:, 42:43], S1[:, 43:44], S1[:, 42:43], ALU.subtract)
            drain("vector")
            rsqrt_small(S1[:, 24 + r:25 + r], S1[:, 42:43], 1, LN_EPS)
            drain("vector")

        b.aux_dma("gpsimd", s_gpd, S2[0:64, 370:371], iota64_d[0:64, 0:1])
        b.aux_dma("gpsimd", s_gpd, A[0:64, 2048:4096],
                  pos_d[None, :].broadcast_to((64, 2 * T)))
        wait("scalar", s_gpd)
        drain("scalar")
        wait("scalar", s_vec)
        act(S2[0:64, 371:372], S2[0:64, 370:371], AF.Exp,
            scale=-float(np.log(10000.0) / 64.0), sem=s_sca)
        wait("vector", s_sca)
        FQ = A[0:64, 2048:4096]
        TK = A[0:64, 4096:6144]
        R1 = A[0:64, 6144:8192]
        cosb = Ah[0:64, 0:2048]
        sinb = Ah[0:64, 2048:4096]
        C1 = 6.28125
        C2 = float(2.0 * np.pi - 6.28125)
        vts(FQ, FQ, S2[0:64, 371:372], None, ALU.mult)
        drain("vector")
        for trig in range(2):
            if trig == 1:
                vts(FQ, FQ, float(np.pi / 2.0), None, ALU.add)
                drain("vector")
            vts(TK, FQ, float(1.0 / (2.0 * np.pi)), None, ALU.mult)
            drain("vector")
            vts(TK, TK, RNE_C, RNE_C, ALU.add, ALU.subtract)
            drain("vector")
            vstt(R1, TK, -C1, FQ, ALU.mult, ALU.add)
            drain("vector")
            vstt(R1, TK, -C2, R1, ALU.mult, ALU.add)
            drain("vector")
            vts(R1, R1, 3.1415925, -3.1415925, ALU.min, ALU.max, sem=s_vec)
            drain("vector")
            drain("scalar")
            wait("scalar", s_vec)
            act(TK, R1, AF.Sin, sem=s_sca)
            wait("vector", s_sca)
            op("vector", "tensor_copy", out=(sinb if trig == 0 else cosb),
               in_=TK, sem=s_vec)
            drain("vector")
        cosD = Ah[:, 4096:6144]
        sinS = Ah[:, 6144:8192]
        op("vector", "tensor_copy", out=cosD[0:64, :], in_=cosb)
        vts(sinS[0:64, :], sinb, -1.0, None, ALU.mult, sem=s_vec)
        drain("vector")
        wait("gpsimd", s_vec)
        b.aux_dma("gpsimd", s_gpd, cosD[64:128, :], cosb)
        b.aux_dma("gpsimd", s_gpd, sinS[64:128, :], sinb)
        wait("vector", s_gpd)

        xlnf = E[:, 0:2048]
        xlh = Dh[:, 0:2048]
        wait("vector", s_ten)
        for r in range(NT):
            vts(xlnf, xt(r), S1[:, 16 + r:17 + r], None, ALU.subtract)
            drain("vector")
            vts(xlnf, xlnf, S1[:, 24 + r:25 + r], None, ALU.mult)
            drain("vector")
            op("vector", "tensor_copy", out=xlh, in_=xlnf, sem=s_vec)
            wait("tensor", s_vec)
            for k in range(16):
                pc = (PT0, PT1)[k % 2]
                if k >= 2:
                    wait("tensor", s_vec)
                transpose(pc[:, 0:128], xlh[:, k * 128:(k + 1) * 128], sem=s_ten)
                wait("vector", s_ten)
                op("vector", "tensor_copy", sem=s_vec,
                   out=Bh[:, k * 1024 + r * 128: k * 1024 + (r + 1) * 128],
                   in_=pc[:, 0:128])
            drain("vector")

        def xlnTb(k):
            return Bh[:, k * 1024:(k + 1) * 1024]

        wstg = [E[:, 0:512], E[:, 512:1024]]
        wbf = [Eh[:, 3072:3584], Eh[:, 3584:4096]]
        wbf_free = {0: 0, 1: 0}

        def stream_wtile(w_d, k, c0, w):
            tok = wst.load([(wstg[wst.i % 2][:, 0:w],
                             w_d[k * 128:(k + 1) * 128, c0:c0 + w])])
            p = tok[0]
            wst.wait_tok("vector", tok)
            wait("vector", s_ten, wbf_free[p])
            op("vector", "tensor_copy", sem=s_vec, out=wbf[p][:, 0:w],
               in_=wstg[p][:, 0:w])
            wst.mark_free(tok, s_vec)
            return p, s_vec.n

        paslot = [PA0[:, 0:512], PA0[:, 512:1024], PA1[:, 0:512], PA1[:, 512:1024]]

        rot_t = Ah[:, 8192:9216]
        tmp_t = Ah[:, 9216:10240]

        def rope(tile):
            op("vector", "tensor_copy", out=tile[0:64, :], in_=tile[0:64, :],
               sem=s_vec)
            drain("vector")
            wait("gpsimd", s_vec)
            b.aux_dma("gpsimd", s_gpd, rot_t[0:64, :], tile[64:128, :])
            b.aux_dma("gpsimd", s_gpd, rot_t[64:128, :], tile[0:64, :])
            wait("vector", s_gpd)
            vtt(tmp_t, rot_t, sinS[:, 1024:2048], ALU.mult)
            drain("vector")
            vtt(tile, tile, cosD[:, 1024:2048], ALU.mult)
            drain("vector")
            vtt(tile, tile, tmp_t, ALU.add, sem=s_vec)
            drain("vector")

        qh = Ah[:, 12288:14336]
        wait("sync", s_vec)
        for wqk in range(2):
            w_d = awk_d if wqk == 0 else awq_d
            for og in range(8):
                wait("tensor", s_vec)
                for dm in range(16):
                    p_, wcnt = stream_wtile(w_d, dm, og * 256, 256)
                    wait("tensor", s_vec, wcnt)
                    for oo in range(2):
                        for th in range(2):
                            mm(paslot[oo * 2 + th],
                               wbf[p_][:, oo * 128:(oo + 1) * 128],
                               xlnTb(dm)[:, th * 512:(th + 1) * 512],
                               start=(dm == 0), stop=(dm == 15),
                               sem=s_ten if (oo == 1 and th == 1) else None)
                    wbf_free[p_] = s_ten.n
                wait("vector", s_ten)
                wait("vector", s_gpd)
                for oo in range(2):
                    h = og * 2 + oo
                    hq_t = qh[:, (oo % 2) * 1024:((oo % 2) + 1) * 1024]
                    for th in range(2):
                        op("vector", "tensor_copy",
                           out=hq_t[:, th * 512:(th + 1) * 512],
                           in_=paslot[oo * 2 + th])
                    drain("vector")
                    rope(hq_t)
                    if wqk == 0:
                        op("vector", "tensor_copy", sem=s_vec,
                           out=Ch[:, h * 1024:(h + 1) * 1024], in_=hq_t)
                        drain("vector")
                    else:
                        wait("gpsimd", s_vec)
                        b.aux_dma("gpsimd", s_gpd,
                                  scr_q.ap()[h * 128:(h + 1) * 128, :], hq_t)

        wait("gpsimd", s_vec)
        for h in range(16):
            b.aux_dma("gpsimd", s_gpd, cck_in.ap()[h * 128:(h + 1) * 128, :],
                      Ch[:, h * 1024:(h + 1) * 1024])

        for rg in range(2):
            for dv in range(4):
                wait("tensor", s_vec)
                for dm in range(16):
                    p_, wcnt = stream_wtile(awv_d, dm, dv * 512, 512)
                    wait("tensor", s_vec, wcnt)
                    for rr in range(4):
                        r = rg * 4 + rr
                        mm(paslot[rr],
                           xlnTb(dm)[:, r * 128:(r + 1) * 128],
                           wbf[p_][:, 0:512],
                           start=(dm == 0), stop=(dm == 15),
                           sem=s_ten if rr == 3 else None)
                    wbf_free[p_] = s_ten.n
                wait("vector", s_ten)
                for rr in range(4):
                    r = rg * 4 + rr
                    op("vector", "tensor_copy", sem=s_vec,
                       out=Dh[:, r * 2048 + dv * 512: r * 2048 + (dv + 1) * 512],
                       in_=paslot[rr])
                drain("vector")
        wait("gpsimd", s_vec)
        for r in range(NT):
            b.aux_dma("gpsimd", s_gpd, ccv_in.ap()[r * 128:(r + 1) * 128, :],
                      Dh[:, r * 2048:(r + 1) * 2048])

        for gi, go in ((cck_in, cck_outF), (ccv_in, ccv_outF)):
            raw("gpsimd", lambda e, gi=gi, go=go: e.collective_compute(
                "AllGather", ALU.bypass, replica_groups=[list(range(NC))],
                ins=[gi.ap()[:, :].opt()],
                outs=[go.ap()[:, :].opt()]).then_inc(s_cc.h, 1))
            s_cc.n += 1
        wait("gpsimd", s_cc)
        for j in range(8):
            b.aux_dma("gpsimd", s_gpd, S3[:, 344 + j:345 + j],
                      nbr_d[None, j:j + 1].broadcast_to((1, 128)))
        stg = [Ah[:, j * 1024:(j + 1) * 1024] for j in range(8)]
        accb = Ah[:, 8192:9216]
        qs = ["sync", "gpsimd", "scalar"]
        qsem = {"sync": s_aux, "gpsimd": s_gpd, "scalar": s_scd}
        wait("sync", s_cc)
        wait("scalar", s_cc)
        wait("sync", s_vec)
        wait("scalar", s_vec)
        for part in range(2):
            for t in range(16):
                for q in qs:
                    wait(q, s_vec)
                for j in range(8):
                    if part == 0:
                        bl = cck_outF.ap()[j * D + t * 128: j * D + (t + 1) * 128, :]
                        dsl = scr_hk.ap()[t * 128:(t + 1) * 128, :]
                    else:
                        r_, hc = t // 2, t % 2
                        bl = ccv_outF.ap()[j * T + r_ * 128: j * T + (r_ + 1) * 128,
                                           hc * 1024:(hc + 1) * 1024]
                        dsl = scr_hv.ap()[r_ * 128:(r_ + 1) * 128,
                                          hc * 1024:(hc + 1) * 1024]
                    q = qs[j % 3]
                    b.aux_dma(q, qsem[q], stg[j], bl)
                for q in qs:
                    wait("vector", qsem[q])
                vts(accb, stg[0], S3[:, 344:345], None, ALU.mult)
                drain("vector")
                for j in range(1, 8):
                    vstt(accb, stg[j], S3[:, 344 + j:345 + j], accb,
                         ALU.mult, ALU.add, sem=s_vec if j == 7 else None)
                    drain("vector")
                wait("gpsimd", s_vec)
                b.aux_dma("gpsimd", s_gpd, dsl, accb)
        halo_ready = s_gpd.n

        onescol = KONh[:, 384:385]
        op("vector", "memset", ap=onescol, constant=1.0)
        op("vector", "memset", ap=S2[:, 381:382], constant=SWA_BIAS, sem=s_vec)
        drain("vector")
        ET = Eh[:, 0:1152]
        msk = Eh[:, 1152:2304]
        qtl = Eh[:, 2304:3328]
        aw_f = E[:, 1664:1792]
        aw_b = Eh[:, 3584:3712]
        wait("scalar", s_gpd, halo_ready)
        wait("scalar", s_vec)
        wait("scalar", s_ten)
        for hh in range(2):
            wait("scalar", s_ten)
            for w8 in range(8):
                b.aux_dma("scalar", s_scd, Ah[:, w8 * 1024:(w8 + 1) * 1024]
                          .rearrange("p (h s) -> p h s", s=128),
                          scr_hk.ap()[hh * 1024:(hh + 1) * 1024,
                                      w8 * 128:(w8 + 1) * 128]
                          .rearrange("(h p) s -> p h s", p=128))
                b.aux_dma("scalar", s_scd,
                          Ah[:, 8192 + w8 * 1024: 8192 + (w8 + 1) * 1024],
                          scr_hv.ap()[w8 * 128:(w8 + 1) * 128,
                                      hh * 1024:(hh + 1) * 1024])
            for qb in range(NT):
                wait("scalar", s_ten)
                b.aux_dma("scalar", s_scd, msk, swamask_d[qb * 128:(qb + 1) * 128, :])
                b.aux_dma("scalar", s_scd,
                          qtl.rearrange("p (h s) -> p h s", s=128),
                          scr_q.ap()[hh * 1024:(hh + 1) * 1024,
                                     qb * 128:(qb + 1) * 128]
                          .rearrange("(h p) s -> p h s", p=128))
                for hl in range(8):
                    h = hh * 8 + hl
                    wait("tensor", s_scd)
                    wait("tensor", s_vec)
                    for j in range(9):
                        w = qb - 8 + j
                        if w < 0:
                            kb = Ah[:, (w + 8) * 1024 + hl * 128:
                                    (w + 8) * 1024 + (hl + 1) * 128]
                        else:
                            kb = Ch[:, h * 1024 + w * 128:
                                    h * 1024 + (w + 1) * 128]
                        dstp = PA0[:, j * 128:(j + 1) * 128] if j < 8                             else PA1[:, 0:128]
                        mm(dstp, kb, qtl[:, hl * 128:(hl + 1) * 128],
                           start=True, stop=True,
                           sem=s_ten if j == 8 else None)
                    wait("scalar", s_ten)
                    drain("scalar")
                    wait("scalar", s_vec)
                    act(ET[:, 0:1024], PA0[:, 0:1024], AF.Exp,
                        scale=float(DK) ** -0.5, bias=S2[:, 381:382], sem=s_sca)
                    drain("scalar")
                    act(ET[:, 1024:1152], PA1[:, 0:128], AF.Exp,
                        scale=float(DK) ** -0.5, bias=S2[:, 381:382], sem=s_sca)
                    wait("vector", s_sca)
                    wait("vector", s_scd)
                    vtt(ET, ET, msk, ALU.mult, sem=s_vec)
                    drain("vector")
                    wait("tensor", s_vec)
                    for j in range(9):
                        w = qb - 8 + j
                        if w < 0:
                            vb = Ah[:, 8192 + (w + 8) * 1024 + hl * 128:
                                    8192 + (w + 8) * 1024 + (hl + 1) * 128]
                        else:
                            vb = Dh[:, w * 2048 + h * 128:
                                    w * 2048 + (h + 1) * 128]
                        mm(PB[:, 0:128], ET[:, j * 128:(j + 1) * 128], vb,
                           start=(j == 0), stop=(j == 8))
                        mm(PB[:, 512:513], ET[:, j * 128:(j + 1) * 128],
                           onescol, start=(j == 0), stop=(j == 8),
                           sem=s_ten if j == 8 else None)
                    wait("vector", s_ten)
                    op("vector", "reciprocal", out=S2[:, 380:381],
                       in_=PB[:, 512:513])
                    drain("vector")
                    vts(aw_f, PB[:, 0:128], S2[:, 380:381], None, ALU.mult)
                    drain("vector")
                    op("vector", "tensor_copy", out=aw_b, in_=aw_f, sem=s_vec)
                    wait("tensor", s_vec)
                    transpose(PT0[:, 0:128], aw_b, sem=s_ten)
                    wait("vector", s_ten)
                    op("vector", "tensor_copy", sem=s_vec,
                       out=Bh[:, h * 1024 + qb * 128: h * 1024 + (qb + 1) * 128],
                       in_=PT0[:, 0:128])
                    drain("vector")

        wait("sync", s_vec)
        wait("vector", s_ten)
        for kq in range(64):
            k, qq = kq // 4, kq % 4
            tok = wst.load([(wstg[wst.i % 2],
                             awo_d[k * 128:(k + 1) * 128,
                                   qq * 512:(qq + 1) * 512])])
            p = tok[0]
            wst.wait_tok("vector", tok)
            op("vector", "tensor_copy", sem=s_vec,
               out=wl0(k)[:, qq * 512:(qq + 1) * 512], in_=wstg[p])
            wst.mark_free(tok, s_vec)
        for r in range(NT):
            for rnd in range(2):
                wait("tensor", s_vec)
                for k in range(16):
                    for fo in range(2):
                        o = rnd * 2 + fo
                        mm(PB[:, fo * 512:(fo + 1) * 512],
                           Bh[:, k * 1024 + r * 128: k * 1024 + (r + 1) * 128],
                           wl0(k)[:, o * 512:(o + 1) * 512],
                           start=(k == 0), stop=(k == 15),
                           sem=s_ten if (k == 15 and fo == 1) else None)
                wait("vector", s_ten)
                for fo in range(2):
                    o = rnd * 2 + fo
                    sl = slice(o * 512, (o + 1) * 512)
                    vtt(xt(r)[:, sl], xt(r)[:, sl], PB[:, fo * 512:(fo + 1) * 512],
                        ALU.add, sem=s_vec if fo == 1 else None)
                drain("vector")

    if NSUB >= 3:
        if NSUB >= 4:
            l1_presum()
        phase_l0ffn("p3")
    if NSUB >= 4:
        phase_l1ffn(True)
    if NSUB >= 5:
        phase_swa()
    if NSUB >= 6:
        phase_l1ffn(False)

    wait("sync", s_vec)
    wait("sync", s_sca)
    wait("sync", s_ten)
    for r in range(NT):
        dma("sync", out_d[r * 128:(r + 1) * 128, :], xt(r), s_out)
    wait("sync", s_out)

    return b.finish()



_NC_CACHE = {}


def _host_consts(core):
    base = core * T
    ident = np.eye(128, dtype=np.float32)
    glamask = np.tril(np.ones((CH, CH), np.float32)).T
    m = np.zeros((NT, 128, 1152), np.float32)
    for qb in range(NT):
        for j in range(9):
            kb_global = base + (qb - 8 + j) * 128
            for p in range(128):
                s_glob = kb_global + p
                if s_glob < 0:
                    continue
                qg0 = base + qb * 128
                lo = s_glob - qg0
                hi = s_glob + WIN - qg0
                lo = max(lo, 0)
                hi = min(hi, 128)
                if hi > lo:
                    m[qb, p, j * 128 + lo:j * 128 + hi] = 1.0
    alphas = np.zeros(8, np.float32)
    alphas[:core] = 1.0
    nbr = np.zeros(8, np.float32)
    if core > 0:
        nbr[core - 1] = 1.0
    oddsel = np.array([1.0, 0.0] if core % 2 == 1 else [0.0, 1.0], np.float32)
    pos = (np.arange(2 * T) + base - T).astype(np.float32)
    iota = np.arange(64).astype(np.float32)
    import ml_dtypes
    return {
        "ident": ident.astype(ml_dtypes.bfloat16),
        "glamask": glamask.astype(ml_dtypes.bfloat16),
        "swamask": np.ascontiguousarray(m.reshape(NT * 128, 1152)).astype(ml_dtypes.bfloat16),
        "alphas": alphas, "oddsel": oddsel, "nbr": nbr, "pos": pos,
        "iota64": iota[:, None].copy(),
    }


def kernel(L0_attn_hidden_states, L0_attn_Wq, L0_attn_Wk, L0_attn_Wi,
           L0_attn_Wo, L0_attn_g_weight, L0_ffn_weight, L0_ffn_weight_scale,
           L0_ffn_bias, L1_attn_wq, L1_attn_wk, L1_attn_wv, L1_attn_wo,
           L1_ffn_w1, L1_ffn_w2, L1_ffn_w3):
    x = np.ascontiguousarray(np.asarray(L0_attn_hidden_states, np.float32).reshape(8192, D))
    l0wT = np.ascontiguousarray(np.asarray(L0_ffn_weight, np.float32).T)
    w1T = np.ascontiguousarray(np.asarray(L1_ffn_w1, np.float32).T)
    w2T = np.ascontiguousarray(np.asarray(L1_ffn_w2, np.float32).T)
    w3T = np.ascontiguousarray(np.asarray(L1_ffn_w3, np.float32).T)
    shared = {
        "l0wT": l0wT,
        "l0ws": np.asarray(L0_ffn_weight_scale, np.float32).reshape(1, 1),
        "l0bias": np.asarray(L0_ffn_bias, np.float32),
        "wq": np.ascontiguousarray(np.asarray(L0_attn_Wq, np.float32)),
        "wk": np.ascontiguousarray(np.asarray(L0_attn_Wk, np.float32)),
        "wi": np.ascontiguousarray(np.asarray(L0_attn_Wi, np.float32)),
        "wo": np.ascontiguousarray(np.asarray(L0_attn_Wo, np.float32)),
        "gw": np.asarray(L0_attn_g_weight, np.float32),
        "awq": np.ascontiguousarray(np.asarray(L1_attn_wq, np.float32)),
        "awk": np.ascontiguousarray(np.asarray(L1_attn_wk, np.float32)),
        "awv": np.ascontiguousarray(np.asarray(L1_attn_wv, np.float32)),
        "awo": np.ascontiguousarray(np.asarray(L1_attn_wo, np.float32)),
        "w1T": w1T, "w2T": w2T, "w3T": w3T,
    }
    key = NSUB
    if key not in _NC_CACHE:
        _NC_CACHE[key] = build_program()
    ncp = _NC_CACHE[key]
    in_maps = []
    for c in range(NC):
        m = dict(shared)
        m["x"] = np.ascontiguousarray(x[c * T:(c + 1) * T])
        m.update(_host_consts(c))
        in_maps.append(m)
    res = run_bass_kernel_spmd(ncp, in_maps, core_ids=list(range(NC)))
    out = np.concatenate([np.asarray(r["out"]) for r in res.results], axis=0)
    return out.reshape(1, 8192, D).astype(np.float32)
